# version 01
# baseline (speedup 1.0000x reference)
import numpy as np
import ml_dtypes
import concourse.bass as bass
import concourse.mybir as mybir
from concourse.bass_utils import run_bass_kernel_spmd
from contextlib import ExitStack

F32 = mybir.dt.float32
BF16 = mybir.dt.bfloat16
AF = mybir.ActivationFunctionType
ALU = mybir.AluOpType
AX = mybir.AxisListType
bf16 = ml_dtypes.bfloat16

B, T, C = 4, 2048, 2048
H, D = 16, 128
HL = 8
TO = 1024
HF = 8192
NT = T // 128
NTO = TO // 128
NC = C // 128
SCALE = float(D) ** -0.5
LN_EPS = 1e-5
RMS_EPS = 1.1920929e-07

DEBUG_TAPS = ()

ENGS = ("sync", "tensor", "vector", "scalar", "gpsimd")


NLANE = 8


class Em:
    def __init__(self, nc):
        self.nc = nc
        self.q = {en: [] for en in ENGS}
        self.clk = {}
        self.dclk = {}
        self.dma_n = {}
        for en in ("tensor", "vector", "scalar", "gpsimd"):
            self.clk[en] = [nc.alloc_semaphore(f"clk_{en}"), 0]
        for en in ("sync", "gpsimd", "vector", "scalar"):
            self.dclk[en] = [[nc.alloc_semaphore(f"dma_{en}{i}"), 0]
                             for i in range(NLANE)]
            self.dma_n[en] = 0
        self.bank_reader = {}

    def tok(self, en):
        return ("c", en, 0, self.clk[en][1])

    def dtoks(self, en):
        return [("d", en, i, self.dclk[en][i][1]) for i in range(NLANE)
                if self.dclk[en][i][1] > 0]

    def _sem_val(self, t):
        kind, en, lane, v = t
        sem = self.clk[en][0] if kind == "c" else self.dclk[en][lane][0]
        return sem, v

    @staticmethod
    def cover(tokens):
        best = {}
        for t in tokens:
            if t is None:
                continue
            k = (t[0], t[1], t[2])
            if k not in best or best[k][3] < t[3]:
                best[k] = t
        return list(best.values())

    def op(self, en, builder, waits=(), inc=True):
        waits = list(waits)
        if en in ("vector", "scalar", "gpsimd") and self.clk[en][1] > 0:
            waits.append(("c", en, 0, self.clk[en][1]))
        if inc:
            self.clk[en][1] += 1
        mytok = ("c", en, 0, self.clk[en][1])
        sem = self.clk[en][0]
        ws = [self._sem_val(t) for t in self.cover(waits) if t[3] > 0]

        def fn(e, _b=builder, _ws=ws, _inc=inc, _sem=sem):
            for s, v in _ws:
                e.wait_ge(s, v)
            ins = _b(e)
            if _inc:
                ins.then_inc(_sem, 1)
        self.q[en].append(fn)
        return mytok

    def dma(self, en, out, in_, waits=()):
        lane = self.dma_n[en] % NLANE
        self.dma_n[en] += 1
        waits = list(waits)
        if self.dclk[en][lane][1] > 0:
            waits.append(("d", en, lane, self.dclk[en][lane][1]))
        self.dclk[en][lane][1] += 16
        mytok = ("d", en, lane, self.dclk[en][lane][1])
        sem = self.dclk[en][lane][0]
        ws = [self._sem_val(t) for t in self.cover(waits) if t[3] > 0]

        def fn(e, _o=out, _i=in_, _ws=ws, _sem=sem):
            for s, v in _ws:
                e.wait_ge(s, v)
            e.dma_start(out=_o, in_=_i).then_inc(_sem, 16)
        self.q[en].append(fn)
        return mytok

    def wait(self, en, tokens):
        ws = [self._sem_val(t) for t in self.cover(tokens) if t[3] > 0]
        if not ws:
            return

        def fn(e, _ws=ws):
            for s, v in _ws:
                e.wait_ge(s, v)
        self.q[en].append(fn)

    def barrier(self):
        toks = [self.tok(en) for en in ("tensor", "vector", "scalar", "gpsimd")]
        for en in ("sync", "gpsimd", "vector", "scalar"):
            toks += self.dtoks(en)
        for en in ENGS:
            mine = [t for t in toks if not (t[1] == en and t[0] == "c")]
            self.wait(en, mine)

    def bank_wait(self, bk):
        r = self.bank_reader.get(bk)
        return [r] if r is not None else []

    def bank_read(self, bk, tok):
        self.bank_reader[bk] = tok

    def finalize(self):
        with self.nc.Block() as block:
            for name in ENGS:
                ops = self.q[name]
                if not ops:
                    continue
                def section(eng, _ops=ops):
                    for f in _ops:
                        f(eng)
                getattr(block, name)(section)


def _consts():
    ident = np.eye(128, dtype=np.float32).astype(bf16)
    ones_col = np.ones((128, 1), np.float32).astype(bf16)
    ones_rowf = np.ones((1, 128), np.float32)
    onesf = np.ones((1, 1), np.float32)
    E1 = np.zeros((128, 2), np.float32)
    E1[0:64, 0] = 1.0
    E1[64:128, 1] = 1.0
    EB = np.ascontiguousarray(E1.T)
    masks = np.zeros((128, 4 * 512), np.float32)
    p = np.arange(128)[:, None]
    f = np.arange(512)[None, :]
    for k in range(4):
        masks[:, 512 * k:512 * (k + 1)] = (p <= f - 128 * k)
    return ident, ones_col, ones_rowf, onesf, E1.astype(bf16), EB.astype(bf16), masks.astype(bf16)


def build_nc(debug_taps=()):
    nc = bass.Bass("TRN2", target_bir_lowering=False, debug=False,
                   dynamic_dma_scratch_size=16384)
    em = Em(nc)

    x_all = nc.dram_tensor("x_all", [T, C], F32, kind="ExternalInput").ap()
    x_own = nc.dram_tensor("x_own", [TO, C], F32, kind="ExternalInput").ap()
    wq_in = nc.dram_tensor("wq", [HL * C, D], F32, kind="ExternalInput").ap()
    wk_in = nc.dram_tensor("wk", [HL * C, D], F32, kind="ExternalInput").ap()
    wv_in = nc.dram_tensor("wv", [C, HL * D], F32, kind="ExternalInput").ap()
    bq_in = nc.dram_tensor("bq_t", [D, HL], F32, kind="ExternalInput").ap()
    bk_in = nc.dram_tensor("bk_t", [D, HL], F32, kind="ExternalInput").ap()
    bv_in = nc.dram_tensor("bv_r", [1, HL * D], F32, kind="ExternalInput").ap()
    wo_in = nc.dram_tensor("wo", [C, C], F32, kind="ExternalInput").ap()
    bo_in = nc.dram_tensor("bo_r", [1, C], F32, kind="ExternalInput").ap()
    wgu_in = nc.dram_tensor("wgu_sh", [256, 2 * HF], F32, kind="ExternalInput").ap()
    wp_in = nc.dram_tensor("wp_sh", [1024, C], F32, kind="ExternalInput").ap()
    m01_in = nc.dram_tensor("m01", [128, 2], F32, kind="ExternalInput").ap()
    out_d = nc.dram_tensor("out", [TO, C], F32, kind="ExternalOutput").ap()

    whin_d = nc.dram_tensor("whin", [384, 2 * HF], BF16, kind="Internal").ap()
    wG_d = nc.dram_tensor("wG", [8 * 384, 2 * HF], BF16, kind="Internal").ap()
    yin_d = nc.dram_tensor("yin", [HL * D, T], BF16, kind="Internal").ap()
    yG_d = nc.dram_tensor("yG", [C, T], BF16, kind="Internal").ap()
    xp_d = nc.dram_tensor("xp_spill", [TO, C], F32, kind="Internal").ap()

    ident_np, onescol_np, onesrowf_np, onesf_np, E1_np, EB_np, masks_np = _consts()
    cident = nc.inline_tensor(ident_np, name="c_ident")
    conescol = nc.inline_tensor(onescol_np, name="c_onescol")
    conesrowf = nc.inline_tensor(onesrowf_np, name="c_onesrowf")
    conesf = nc.inline_tensor(onesf_np, name="c_onesf")
    cE1 = nc.inline_tensor(E1_np, name="c_E1")
    cEB = nc.inline_tensor(EB_np, name="c_EB")
    cmasks = nc.inline_tensor(masks_np, name="c_masks")

    taps = {}

    def tap(name, shape, dtype=F32):
        if name in debug_taps:
            taps[name] = nc.dram_tensor("tap_" + name, list(shape), dtype,
                                        kind="ExternalOutput").ap()
        return taps.get(name)

    ctx = ExitStack()
    e = ctx.enter_context

    identb = e(nc.sbuf_tensor("identb", [128, 128], BF16))
    onescol = e(nc.sbuf_tensor("onescol", [128, 1], BF16))
    onesrowf = e(nc.sbuf_tensor("onesrowf", [1, 128], F32))
    onesf_s = e(nc.sbuf_tensor("onesf", [1, 1], F32))
    E1b = e(nc.sbuf_tensor("E1b", [128, 2], BF16))
    EBb = e(nc.sbuf_tensor("EBb", [2, 128], BF16))
    masksb = e(nc.sbuf_tensor("masksb", [128, 4 * 512], BF16))
    m01_s = e(nc.sbuf_tensor("m01_sb", [128, 2], F32))
    bqs_s = e(nc.sbuf_tensor("bqs", [128, HL], F32))
    bk_s = e(nc.sbuf_tensor("bk", [128, HL], F32))
    bvf_s = e(nc.sbuf_tensor("bvf", [128, HL * D], F32))
    bof_s = e(nc.sbuf_tensor("bof", [128, C], F32))
    tiny = e(nc.sbuf_tensor("tiny", [128, 24], F32))
    rows_ctx = ExitStack()
    bvrow = rows_ctx.enter_context(nc.sbuf_tensor("bvrow", [1, HL * D], F32))
    borow = rows_ctx.enter_context(nc.sbuf_tensor("borow", [1, C], F32))

    P = [e(nc.psum_tensor(f"p{i}", [128, 512], F32)) for i in range(8)]

    const_toks = []
    for dst, src_ in ((identb[:], cident.ap()), (onescol[:], conescol.ap()),
                      (onesrowf[:], conesrowf.ap()), (onesf_s[:], conesf.ap()),
                      (E1b[:], cE1.ap()), (EBb[:], cEB.ap()),
                      (masksb[:], cmasks.ap()), (m01_s[:], m01_in),
                      (bqs_s[:], bq_in), (bk_s[:], bk_in),
                      (bvrow[:], bv_in), (borow[:], bo_in)):
        const_toks.append(em.dma("sync", dst, src_))
    t_c_all = em.cover(const_toks)
    em.op("vector", lambda v: v.tensor_scalar_mul(bqs_s[:], bqs_s[:], SCALE),
          waits=t_c_all)
    t_bias_mms = []
    for j in range(2):
        tm = em.op("tensor", lambda t, _j=j: t.matmul(
            P[0][:, 0:512], onesrowf[:], bvrow[0:1, 512 * _j:512 * (_j + 1)],
            start=True, stop=True), waits=t_c_all + em.bank_wait(0))
        t_bias_mms.append(tm)
        tv = em.op("vector", lambda v, _j=j: v.tensor_copy(
            bvf_s[:, 512 * _j:512 * (_j + 1)], P[0][:, 0:512]), waits=[tm])
        em.bank_read(0, tv)
    for j in range(4):
        tm = em.op("tensor", lambda t, _j=j: t.matmul(
            P[1][:, 0:512], onesrowf[:], borow[0:1, 512 * _j:512 * (_j + 1)],
            start=True, stop=True), waits=t_c_all + em.bank_wait(1))
        t_bias_mms.append(tm)
        tv = em.op("vector", lambda v, _j=j: v.tensor_copy(
            bof_s[:, 512 * _j:512 * (_j + 1)], P[1][:, 0:512]), waits=[tm])
        em.bank_read(1, tv)

    rows_ctx.close()

    with ExitStack() as qctx:
        qe = qctx.enter_context
        wf = [qe(nc.sbuf_tensor(f"q_wf{i}", [128, 4096], F32)) for i in range(2)]
        ab2 = qe(nc.sbuf_tensor("q_ab2", [128, 4096], BF16))
        sg = qe(nc.sbuf_tensor("q_sg", [128, 4096], BF16))
        sbc = qe(nc.sbuf_tensor("q_sbc", [128, 4096], BF16))
        qt1 = qe(nc.sbuf_tensor("q_t1", [128, 4096], BF16))
        qt2 = [qe(nc.sbuf_tensor(f"q_t2{i}", [128, 4096], BF16)) for i in range(2)]
        scb = qe(nc.sbuf_tensor("q_scb", [2, 4096], BF16))

        panels = []
        for cc in range(2):
            for pp in range(4):
                panels.append((wgu_in[128 * cc:128 * (cc + 1), 4096 * pp:4096 * (pp + 1)],
                               whin_d[128 * cc:128 * (cc + 1), 4096 * pp:4096 * (pp + 1)],
                               4096))
        for kk in range(8):
            dst = whin_d[256 + 16 * kk:256 + 16 * (kk + 1), :].rearrange(
                "a (b c) -> (a b) c", b=8, c=2048)
            panels.append((wp_in[128 * kk:128 * (kk + 1), :], dst, 2048))

        hist = []
        for pi, (src, dst, W) in enumerate(panels):
            buf = pi % 2
            h = {}
            war = ([hist[pi - 2]["sg"], hist[pi - 2]["ab2"]] if pi >= 2
                   else [t_bias_mms[-1]])
            t_ld = em.dma("sync", wf[buf][:, 0:W], src, waits=war)
            prev = hist[-1] if hist else None
            em.op("vector", lambda v, _b=buf, _W=W: v.tensor_scalar_mul(
                qt1[:, 0:_W], wf[_b][:, 0:_W], 2.0),
                waits=[t_ld] + ([prev["qs"]] if prev else []))
            h["ab2"] = em.op("vector", lambda v, _W=W: v.scalar_tensor_tensor(
                ab2[:, 0:_W], qt1[:, 0:_W], -1.0, qt1[:, 0:_W],
                ALU.mult, ALU.max))
            h["sg"] = em.op("scalar", lambda s, _b=buf, _W=W: s.activation(
                sg[:, 0:_W], wf[_b][:, 0:_W], AF.Sign),
                waits=[t_ld] + ([prev["what"]] if prev else []))
            nseg = W // 512
            t_bc_list = []
            for sgi in range(nseg):
                sl = slice(512 * sgi, 512 * (sgi + 1))
                bk_ = sgi % 2
                tm = em.op("tensor", lambda t, _sl=sl, _bk=bk_: t.matmul(
                    P[_bk][0:2, 0:512], E1b[:], ab2[:, _sl], start=True, stop=True),
                    waits=[h["ab2"]] + em.bank_wait(bk_))
                t_sc = em.op("vector", lambda v, _sl=sl, _bk=bk_: v.tensor_scalar(
                    scb[0:2, _sl], P[_bk][0:2, 0:512], 1.0 / 128.0, 1e-8,
                    ALU.mult, ALU.max), waits=[tm])
                em.bank_read(bk_, t_sc)
                bk2 = 2 + sgi % 2
                tm2 = em.op("tensor", lambda t, _sl=sl, _bk=bk2: t.matmul(
                    P[_bk][:, 0:512], EBb[:], scb[0:2, _sl], start=True, stop=True),
                    waits=[t_sc] + em.bank_wait(bk2))
                t_bc = em.op("vector", lambda v, _sl=sl, _bk=bk2: v.tensor_copy(
                    sbc[:, _sl], P[_bk][:, 0:512]), waits=[tm2])
                em.bank_read(bk2, t_bc)
                t_bc_list.append(t_bc)
            h["qa"] = em.op("vector", lambda v, _W=W: v.tensor_tensor(
                qt1[:, 0:_W], ab2[:, 0:_W], sbc[:, 0:_W], ALU.is_gt),
                waits=[t_bc_list[-1], h["ab2"]])
            h["qs"] = em.op("vector", lambda v, _W=W: v.tensor_tensor(
                qt1[:, 0:_W], qt1[:, 0:_W], sbc[:, 0:_W], ALU.mult),
                waits=[h["qa"]])
            st_war = [hist[pi - 2]["st"]] if pi >= 2 else []
            h["what"] = em.op("vector", lambda v, _b=buf, _W=W: v.tensor_tensor(
                qt2[_b][:, 0:_W], qt1[:, 0:_W], sg[:, 0:_W], ALU.mult),
                waits=[h["sg"], h["qs"]] + st_war)
            h["st"] = em.dma("gpsimd", dst, qt2[buf][:, 0:W], waits=[h["what"]])
            hist.append(h)

        t_ccW = em.op("gpsimd", lambda g: g.collective_compute(
            "AllGather", ALU.bypass, replica_groups=[list(range(8))],
            ins=[whin_d], outs=[wG_d]),
            waits=[hist[-1]["st"], hist[-2]["st"]])

    em.barrier()

    laq = ExitStack()
    x1T = laq.enter_context(nc.sbuf_tensor("x1T", [128, NC * T], BF16))
    yT = laq.enter_context(nc.sbuf_tensor("yT", [128, HL * T], BF16))
    x1T_v = x1T.ap().rearrange("p (k t) -> p k t", k=NC, t=T)

    def emit_ln_tile(src_tile, sqd, col0, waits0, war=()):
        o = col0
        war = list(war)
        t_sum = em.op("vector", lambda v, _s=src_tile: v.tensor_reduce(
            tiny[:, o + 0:o + 1], _s, AX.X, ALU.add), waits=list(waits0) + war)
        t_negm = em.op("vector", lambda v: v.tensor_scalar_mul(
            tiny[:, o + 1:o + 2], tiny[:, o + 0:o + 1], -1.0 / C),
            waits=[t_sum])
        t_sq = em.op("scalar", lambda s, _s=src_tile, _d=sqd: s.activation(
            _d, _s, AF.Square, accum_out=tiny[:, o + 5:o + 6]),
            waits=list(waits0) + war)
        t_m2 = em.op("scalar", lambda s: s.activation(
            tiny[:, o + 2:o + 3], tiny[:, o + 1:o + 2], AF.Square),
            waits=[t_negm])
        t_v1 = em.op("vector", lambda v: v.scalar_tensor_tensor(
            tiny[:, o + 3:o + 4], tiny[:, o + 5:o + 6], 1.0 / C,
            tiny[:, o + 2:o + 3], ALU.mult, ALU.subtract),
            waits=[t_m2, t_sq])
        t_v2 = em.op("vector", lambda v: v.tensor_scalar_add(
            tiny[:, o + 3:o + 4], tiny[:, o + 3:o + 4], LN_EPS),
            waits=[t_v1])
        t_ln = em.op("scalar", lambda s: s.activation(
            tiny[:, o + 6:o + 7], tiny[:, o + 3:o + 4], AF.Ln), waits=[t_v2])
        t_rstd = em.op("scalar", lambda s: s.activation(
            tiny[:, o + 4:o + 5], tiny[:, o + 6:o + 7], AF.Exp, scale=-0.5),
            waits=[t_ln])
        return t_negm, t_rstd

    def emit_transpose16(src_b, diag_b, dst_view, i, t_src, t_diag, banks=(4, 5)):
        out_toks = []
        for k4 in range(4):
            bk_ = banks[k4 % 2]
            tmm = None
            for kk in range(4):
                k = 4 * k4 + kk
                w0 = ([t_src, t_diag] + em.bank_wait(bk_)) if kk == 0 else []
                tmm = em.op("tensor", lambda t, _k=k, _kk=kk, _bk=bk_, _s=src_b,
                            _dg=diag_b: t.matmul(
                    P[_bk][:, 128 * _kk:128 * (_kk + 1)],
                    _s[:, 128 * _k:128 * (_k + 1)], _dg[:],
                    start=True, stop=True), waits=w0)
            dst = dst_view[:, 4 * k4:4 * (k4 + 1), 128 * i:128 * (i + 1)]
            srcp = P[bk_].ap().rearrange("p (k t) -> p k t", k=4, t=128)
            tv = em.op("vector", lambda v, _d=dst, _sp=srcp: v.tensor_copy(_d, _sp),
                       waits=[tmm])
            em.bank_read(bk_, tv)
            out_toks.append(tv)
        return out_toks[-1]

    with ExitStack() as lctx:
        le = lctx.enter_context
        xt = [le(nc.sbuf_tensor(f"l_x{i}", [128, C], F32)) for i in range(2)]
        sqd = le(nc.sbuf_tensor("l_sqd", [128, C], F32))
        xb = [le(nc.sbuf_tensor(f"l_xb{i}", [128, C], BF16)) for i in range(2)]
        diag = [le(nc.sbuf_tensor(f"l_diag{i}", [128, 128], BF16)) for i in range(2)]

        last_use = {}
        tiny_war = {}
        t_x1T = None
        for i in range(NT):
            buf = i % 2
            war = last_use.get(buf, [])
            t_x = em.dma("sync", xt[buf][:], x_all[128 * i:128 * (i + 1), :],
                         waits=war)
            t_negm, t_rstd = emit_ln_tile(xt[buf][:], sqd[:], 8 * buf, [t_x],
                                          war=tiny_war.get(buf, []))
            t_xb = em.op("scalar", lambda s, _b=buf, _o=8 * buf: s.activation(
                xb[_b][:], xt[_b][:], AF.Identity, bias=tiny[:, _o + 1:_o + 2]))
            t_dg = em.op("vector", lambda v, _b=buf, _o=8 * buf: v.tensor_scalar_mul(
                diag[_b][:], identb[:], tiny[:, _o + 4:_o + 5]), waits=[t_rstd])
            t_x1T = emit_transpose16(xb[buf][:], diag[buf][:], x1T_v, i, t_xb, t_dg)
            last_use[buf] = [t_xb, t_x1T]
            tiny_war[buf] = [t_xb, t_dg]

    tp = tap("x1T", [NC * 128, T])
    if tp is not None:
        em.dma("gpsimd", tp.rearrange("(k p) t -> p k t", p=128),
               x1T.ap().rearrange("p (k t) -> p k t", k=NC, t=T),
               waits=[t_x1T])

    em.barrier()

    tap("aPT", [128, 2048], BF16)
    tap("aPTM", [128, 2048], BF16)
    tap("ar2", [1, 512], F32)
    tap("aysc", [128, 512], BF16)
    tap("aqb", [128, T], BF16)
    tap("akb", [128, T], BF16)
    tap("avq", [128, NT * 512], BF16)
    with ExitStack() as actx:
        ae = actx.enter_context
        vq = ae(nc.sbuf_tensor("a_vq", [128, NT * 512], BF16))
        wvf = [ae(nc.sbuf_tensor(f"a_wvf{i}", [128, 512], F32)) for i in range(2)]
        wvb = ae(nc.sbuf_tensor("a_wvb", [128, NC * 512], BF16))
        stg = [ae(nc.sbuf_tensor(f"a_stg{i}", [128, 512], F32)) for i in range(2)]
        wqb = ae(nc.sbuf_tensor("a_wqb", [128, 2048], BF16))
        wkb = ae(nc.sbuf_tensor("a_wkb", [128, 2048], BF16))
        qb = [ae(nc.sbuf_tensor("a_qb0", [128, T], BF16))] * 2
        kb = [ae(nc.sbuf_tensor("a_kb0", [128, T], BF16))] * 2
        PT = [ae(nc.sbuf_tensor("a_PT0", [128, 12 * 512], BF16)),
              ae(nc.sbuf_tensor("a_PT1", [128, 16 * 512], BF16))]
        PTM = ae(nc.sbuf_tensor("a_PTM", [128, 4 * 512], BF16))
        r2row = ae(nc.sbuf_tensor("a_r2row", [1, 512], F32))
        r2col = ae(nc.sbuf_tensor("a_r2col", [128, 4], F32))
        ysc = ae(nc.sbuf_tensor("a_ysc", [128, 512], BF16))

        S = {"vq_rd": None, "pt_war": {}, "ptm_war": None, "sum_war": None,
             "r2_war": None, "ysc_war": None, "yt_war": None,
             "y_war": {}, "qk_rd": {}, "wv_cast": []}

        def emit_vquad(qd, extra_wait=None, stage_war=()):
            casts = []
            for k in range(NC):
                buf = k % 2
                war = [casts[k - 2]] if k >= 2 else list(stage_war)
                t_ld = em.dma("sync", wvf[buf][:],
                              wv_in[128 * k:128 * (k + 1), 512 * qd:512 * (qd + 1)],
                              waits=war)
                tc = em.op("vector", lambda v, _b=buf, _k=k: v.tensor_copy(
                    wvb[:, 512 * _k:512 * (_k + 1)], wvf[_b][:]), waits=[t_ld])
                casts.append(tc)
            t_wv = casts[-1]
            S["wv_cast"] = casts[-2:]
            t_last = None
            for j in range(NT):
                tmm = None
                for k in range(NC):
                    w0 = []
                    if k == 0:
                        w0 = [t_wv] + em.bank_wait(7)
                        if j == 0 and extra_wait is not None:
                            w0.append(extra_wait)
                    tmm = em.op("tensor", lambda t, _j=j, _k=k: t.matmul(
                        P[7][:, 0:512],
                        x1T[:, T * _k + 128 * _j: T * _k + 128 * (_j + 1)],
                        wvb[:, 512 * _k:512 * (_k + 1)],
                        start=(_k == 0), stop=(_k == NC - 1)), waits=w0)
                t_last = em.op("vector", lambda v, _j=j, _qd=qd: v.scalar_tensor_tensor(
                    vq[:, 512 * _j:512 * (_j + 1)], P[7][:, 0:512], 1.0,
                    bvf_s[:, 512 * _qd:512 * (_qd + 1)], ALU.mult, ALU.add),
                    waits=[tmm])
                em.bank_read(7, t_last)
            return t_last

        def emit_qk(hl, buf):
            t_casts = {}
            for wsel, (w_in_, wb_dst) in enumerate(((wq_in, wqb), (wk_in, wkb))):
                for qtr in range(4):
                    sb = qtr % 2
                    r0 = hl * C + 512 * qtr
                    t_l = em.dma(
                        "sync",
                        stg[sb].ap().rearrange("p (k d) -> p k d", k=4, d=D),
                        w_in_[r0:r0 + 512, :].rearrange("(k p) d -> p k d", p=128),
                        waits=S["qk_rd"].get(("stg", sb), []))
                    tc = em.op("vector", lambda v, _sb=sb, _d=wb_dst, _q=qtr: v.tensor_copy(
                        _d[:, 512 * _q:512 * (_q + 1)], stg[_sb][:]),
                        waits=[t_l] + (S["qk_rd"].get(("wb", wsel), [])
                                       if qtr == 0 else []))
                    S["qk_rd"][("stg", sb)] = [tc]
                    t_casts[wsel] = tc
            t_cq = t_casts[0]
            t_ck = t_casts[1]
            t_out = None
            for isq, (wb_, outb, t_cc) in enumerate(((wqb, qb[buf], t_cq),
                                                     (wkb, kb[buf], t_ck))):
                for j in range(4):
                    bk_ = 4 + (j % 2)
                    tmm = None
                    for k in range(NC):
                        w0 = ([t_cc] + em.bank_wait(bk_) +
                              S["qk_rd"].get("qk", [])) if k == 0 else []
                        tmm = em.op("tensor", lambda t, _wb=wb_, _j=j, _k=k, _bk=bk_: t.matmul(
                            P[_bk][:, 0:512],
                            _wb[:, 128 * _k:128 * (_k + 1)],
                            x1T[:, T * _k + 512 * _j: T * _k + 512 * (_j + 1)],
                            start=(_k == 0), stop=(_k == NC - 1)), waits=w0)
                    if isq == 0:
                        tv = em.op("vector", lambda v, _o=outb, _j=j, _bk=bk_, _h=hl: v.tensor_scalar(
                            _o[:, 512 * _j:512 * (_j + 1)], P[_bk][:, 0:512],
                            SCALE, bqs_s[:, _h:_h + 1], ALU.mult, ALU.add),
                            waits=[tmm])
                    else:
                        tv = em.op("vector", lambda v, _o=outb, _j=j, _bk=bk_, _h=hl: v.tensor_scalar_add(
                            _o[:, 512 * _j:512 * (_j + 1)], P[_bk][:, 0:512],
                            bk_s[:, _h:_h + 1]), waits=[tmm])
                    em.bank_read(bk_, tv)
                    t_out = tv
                S["qk_rd"][("wb", isq)] = [tmm]
            return t_out

        def emit_attn(hl, buf, t_qk, t_v):
            hq = hl % 4
            scores_rd = []
            for c in range(4):
                pt_id = c % 2
                ptb = PT[pt_id]
                njt = 4 * c + 4
                t_exp = {}
                t_msk = {}
                for j in range(njt):
                    bk_ = j % 2
                    w0 = em.bank_wait(bk_)
                    if j == 0:
                        w0 = w0 + [t_qk] + S["pt_war"].get(pt_id, [])
                    tmm = em.op("tensor", lambda t, _j=j, _c=c, _bk=bk_, _b=buf: t.matmul(
                        P[_bk][:, 0:512],
                        kb[_b][:, 128 * _j:128 * (_j + 1)],
                        qb[_b][:, 512 * _c:512 * (_c + 1)],
                        start=True, stop=True), waits=w0)
                    scores_rd.append(tmm)
                    te = em.op("scalar", lambda s, _j=j, _bk=bk_, _pt=ptb: s.activation(
                        _pt[:, 512 * _j:512 * (_j + 1)], P[_bk][:, 0:512], AF.Exp),
                        waits=[tmm])
                    em.bank_read(bk_, te)
                    t_exp[j] = te
                    if j >= 4 * c:
                        kv = j - 4 * c
                        wm = [t_exp[j]]
                        if S["ptm_war"] is not None and kv == 0:
                            wm.append(S["ptm_war"])
                        tmk = em.op("vector", lambda v, _j=j, _kv=kv, _pt=ptb: v.tensor_tensor(
                            PTM[:, 512 * _kv:512 * (_kv + 1)],
                            _pt[:, 512 * _j:512 * (_j + 1)],
                            masksb[:, 512 * _kv:512 * (_kv + 1)], ALU.mult),
                            waits=wm)
                        t_msk[j] = tmk
                ybk = 3 + (c % 2)
                t_sumpv = None
                for j in range(njt):
                    if j >= 4 * c:
                        srcPT, off, wj = PTM, 512 * (j - 4 * c), [t_msk[j]]
                    else:
                        srcPT, off, wj = ptb, 512 * j, [t_exp[j]]
                    w0 = wj if j > 0 else wj + em.bank_wait(2) + (
                        [S["sum_war"]] if S["sum_war"] else [])
                    em.op("tensor", lambda t, _s=srcPT, _o=off, _j=j, _n=njt: t.matmul(
                        P[2][0:1, 0:512], onescol[:], _s[:, _o:_o + 512],
                        start=(_j == 0), stop=(_j == _n - 1),
                        skip_group_check=True), waits=w0)
                for tt in range(4):
                    for j in range(njt):
                        if j >= 4 * c:
                            srcPT, off = PTM, 512 * (j - 4 * c)
                        else:
                            srcPT, off = ptb, 512 * j
                        w1 = []
                        if j == 0 and tt == 0:
                            w1 = [t_v] + em.bank_wait(ybk)
                        t_sumpv = em.op("tensor", lambda t, _s=srcPT, _o=off, _tt=tt,
                                        _j=j, _n=njt, _bk=ybk, _hq=hq: t.matmul(
                            P[_bk][:, 128 * _tt:128 * (_tt + 1)],
                            _s[:, _o + 128 * _tt:_o + 128 * (_tt + 1)],
                            vq[:, 512 * _j + 128 * _hq: 512 * _j + 128 * (_hq + 1)],
                            start=(_j == 0), stop=(_j == _n - 1),
                            skip_group_check=True), waits=w1)
                S["pt_war"][pt_id] = [t_sumpv]
                S["ptm_war"] = t_sumpv
                t_r2 = em.op("vector", lambda v: v.reciprocal(
                    r2row[0:1, :], P[2][0:1, 0:512]), waits=[t_sumpv])
                em.bank_read(2, t_r2)
                t_r2t = None
                for tt in range(4):
                    w0 = []
                    if tt == 0:
                        w0 = [t_r2] + em.bank_wait(5)
                    t_r2t = em.op("tensor", lambda t, _tt=tt: t.matmul(
                        P[5][:, _tt:_tt + 1],
                        r2row[0:1, 128 * _tt:128 * (_tt + 1)], onesf_s[:],
                        start=True, stop=True), waits=w0)
                t_r2c = em.op("vector", lambda v: v.tensor_copy(
                    r2col[:, 0:4], P[5][:, 0:4]), waits=[t_r2t])
                em.bank_read(5, t_r2c)
                t_ysc = None
                for tt in range(4):
                    w0 = [t_r2c]
                    if tt == 0 and S["ysc_war"]:
                        w0 = w0 + [S["ysc_war"]]
                    t_ysc = em.op("vector", lambda v, _tt=tt, _bk=ybk: v.tensor_scalar_mul(
                        ysc[:, 128 * _tt:128 * (_tt + 1)],
                        P[_bk][:, 128 * _tt:128 * (_tt + 1)],
                        r2col[:, _tt:_tt + 1]), waits=w0)
                em.bank_read(ybk, t_ysc)
                t_yt = None
                for tt in range(4):
                    w0 = []
                    if tt == 0:
                        w0 = [t_ysc] + em.bank_wait(6)
                    t_yt = em.op("tensor", lambda t, _tt=tt: t.matmul(
                        P[6][:, 128 * _tt:128 * (_tt + 1)],
                        ysc[:, 128 * _tt:128 * (_tt + 1)], identb[:],
                        start=True, stop=True), waits=w0)
                S["ysc_war"] = t_yt
                t_ye = em.op("vector", lambda v, _hl=hl, _c=c: v.tensor_copy(
                    yT[:, T * _hl + 512 * _c: T * _hl + 512 * (_c + 1)],
                    P[6][:, 0:512]), waits=[t_yt])
                em.bank_read(6, t_ye)
                S["sum_war"] = t_r2c
                if hl == 0 and c == 0 and "aPT" in debug_taps:
                    tdm1 = em.dma("gpsimd", taps["aPT"], PT[0].ap()[:, 0:2048],
                                  waits=[t_exp[3]])
                    tdm2 = em.dma("gpsimd", taps["aPTM"], PTM.ap()[:, 0:2048],
                                  waits=[t_msk[3]])
                    tdm3 = em.dma("gpsimd", taps["ar2"], r2row.ap()[0:1, :],
                                  waits=[t_r2])
                    tdm4 = em.dma("gpsimd", taps["aysc"], ysc.ap()[:, 0:512],
                                  waits=[t_ysc])
                    tdm5 = em.dma("gpsimd", taps["aqb"], qb[0].ap()[:, :],
                                  waits=[t_qk])
                    tdm6 = em.dma("gpsimd", taps["akb"], kb[0].ap()[:, :],
                                  waits=[t_qk])
                    tdm7 = em.dma("gpsimd", taps["avq"], vq.ap()[:, :],
                                  waits=[t_v])
                    S["ptm_war"] = tdm2
                    S["pt_war"][0] = [t_sumpv, tdm1]
                    S["sum_war"] = tdm3
                    S["ysc_war"] = tdm4
            S["qk_rd"]["qk"] = [scores_rd[-1]]
            return t_ye, t_sumpv

        t_v = emit_vquad(0)
        t_qk = emit_qk(0, 0)
        t_vq_rd = None
        t_ye = None
        for hl in range(HL):
            t_ye, t_pv = emit_attn(hl, hl % 2, t_qk, t_v)
            t_vq_rd = t_pv
            if hl == 3:
                t_v = emit_vquad(1, extra_wait=t_vq_rd, stage_war=S["wv_cast"])
            if hl + 1 < HL:
                t_qk = emit_qk(hl + 1, (hl + 1) % 2)

    em.barrier()

    tp = tap("yT", [HL * 128, T])
    if tp is not None:
        em.dma("gpsimd", tp.rearrange("(h p) t -> p h t", p=128),
               yT.ap().rearrange("p (h t) -> p h t", h=HL, t=T))

    t_yst = em.dma("gpsimd", yin_d.rearrange("(h p) t -> p h t", p=128),
                   yT.ap().rearrange("p (h t) -> p h t", h=HL, t=T))
    em.barrier()
    laq.close()
    t_ccY = em.op("gpsimd", lambda g: g.collective_compute(
        "AllGather", ALU.bypass, replica_groups=[[0, 1], [2, 3], [4, 5], [6, 7]],
        ins=[yin_d], outs=[yG_d]), waits=[t_yst])
    tp = tap("yin", [HL * D, T], BF16)
    if tp is not None:
        em.dma("gpsimd", tp, yin_d, waits=[t_yst])
    tp = tap("yG", [C, T], BF16)
    if tp is not None:
        em.dma("gpsimd", tp, yG_d, waits=[t_ccY])

    tap("ysel", [NC * 128, TO], BF16)
    tap("g00", [128, TO], BF16)
    tap("g10", [128, TO], BF16)
    tap("xb20", [128, C], BF16)
    tap("diag20", [128, 128], BF16)
    tap("tinyw", [128, 24])
    tap("identw", [128, 128], BF16)
    tap("att0", [128, C])
    tap("att1", [128, C])
    x2T = e(nc.sbuf_tensor("x2T", [128, NC * TO], BF16))
    x2T_v = x2T.ap().rearrange("p (k t) -> p k t", k=NC, t=TO)

    t_xp_spill = []
    with ExitStack() as wctx:
        we = wctx.enter_context
        wof = [we(nc.sbuf_tensor(f"w_wof{i}", [128, C], F32)) for i in range(2)]
        wob = [we(nc.sbuf_tensor(f"w_wob{i}", [128, C], BF16)) for i in range(2)]
        g0 = [we(nc.sbuf_tensor(f"w_g0{i}", [128, TO], BF16)) for i in range(2)]
        g1 = [we(nc.sbuf_tensor(f"w_g1{i}", [128, TO], BF16)) for i in range(2)]
        gt = we(nc.sbuf_tensor("w_gt", [128, TO], BF16))
        ysel = we(nc.sbuf_tensor("w_ysel", [128, NC * TO], BF16))
        att = [we(nc.sbuf_tensor(f"w_att{i}", [128, C], F32)) for i in range(2)]
        xo = [we(nc.sbuf_tensor(f"w_xo{i}", [128, C], F32)) for i in range(2)]
        xp = [we(nc.sbuf_tensor(f"w_xp{i}", [128, C], F32)) for i in range(2)]
        sqd2 = we(nc.sbuf_tensor("w_sqd", [128, C], F32))
        xb2 = [we(nc.sbuf_tensor(f"w_xb2{i}", [128, C], BF16)) for i in range(4)]
        diag2 = [we(nc.sbuf_tensor(f"w_diag2{i}", [128, 128], BF16)) for i in range(4)]

        em.wait("gpsimd", [t_ccY])
        gsel = []
        for k in range(NC):
            buf = k % 2
            war = [gsel[k - 2]] if k >= 2 else []
            t0 = em.dma("gpsimd", g0[buf][:], yG_d[128 * k:128 * (k + 1), 0:TO],
                        waits=war)
            t1 = em.dma("gpsimd", g1[buf][:], yG_d[128 * k:128 * (k + 1), TO:T])
            if k == 0 and "g00" in debug_taps:
                tg1 = em.dma("gpsimd", taps["g00"], g0[0].ap()[:, :], waits=[t0])
                tg2 = em.dma("gpsimd", taps["g10"], g1[0].ap()[:, :], waits=[t1])
            tgt = em.op("vector", lambda v, _b=buf: v.tensor_scalar_mul(
                gt[:], g0[_b][:], m01_s[:, 0:1]), waits=[t0])
            ts = em.op("vector", lambda v, _b=buf, _k=k: v.scalar_tensor_tensor(
                ysel[:, TO * _k:TO * (_k + 1)], g1[_b][:], m01_s[:, 1:2], gt[:],
                ALU.mult, ALU.add), waits=[t1])
            gsel.append(ts)
        t_ysel = gsel[-1]
        if "ysel" in debug_taps:
            em.dma("gpsimd", taps["ysel"].rearrange("(k p) t -> p k t", p=128),
                   ysel.ap().rearrange("p (k t) -> p k t", k=NC, t=TO),
                   waits=[t_ysel])

        last_w = {}
        wo_war = {}
        for pr in range(4):
            t_wo_mm = None
            for k in range(NC):
                ws = k % 2
                t_l = em.dma("sync", wof[ws][:], wo_in[128 * k:128 * (k + 1), :],
                             waits=wo_war.get(("f", ws), []))
                t_cw = em.op("vector", lambda v, _w=ws: v.tensor_copy(
                    wob[_w][:], wof[_w][:]),
                    waits=[t_l] + wo_war.get(("b", ws), []))
                wo_war[("f", ws)] = [t_cw]
                for pi2 in range(2):
                    i = 2 * pr + pi2
                    for ec in range(4):
                        bk_ = 4 * pi2 + ec
                        w0 = [t_cw]
                        if k == 0:
                            w0 = w0 + [t_ysel] + em.bank_wait(bk_)
                        t_wo_mm = em.op("tensor", lambda t, _i=i, _ec=ec, _k=k,
                                        _bk=bk_, _w=ws: t.matmul(
                            P[_bk][:, 0:512],
                            ysel[:, TO * _k + 128 * _i: TO * _k + 128 * (_i + 1)],
                            wob[_w][:, 512 * _ec: 512 * (_ec + 1)],
                            start=(_k == 0), stop=(_k == NC - 1),
                            skip_group_check=True), waits=w0)
                wo_war[("b", ws)] = [t_wo_mm]
            t_att_p = {}
            for pi2 in range(2):
                buf = pi2
                t_att = None
                for ec in range(4):
                    bk_ = 4 * pi2 + ec
                    t_att = em.op("vector", lambda v, _b=buf, _ec=ec, _bk=bk_: v.scalar_tensor_tensor(
                        att[_b][:, 512 * _ec:512 * (_ec + 1)], P[_bk][:, 0:512], 1.0,
                        bof_s[:, 512 * _ec:512 * (_ec + 1)], ALU.mult, ALU.add),
                        waits=[t_wo_mm] + (last_w.get(("att", buf), [])
                                           if ec == 0 else []))
                    em.bank_read(bk_, t_att)
                t_att_p[pi2] = t_att
            for pi2 in range(2):
                i = 2 * pr + pi2
                buf = pi2
                dbuf = 2 * (pr % 2) + pi2
                o = 8 * buf
                t_att = t_att_p[pi2]
                if pr == 0 and ("att" + str(pi2)) in debug_taps:
                    em.dma("gpsimd", taps["att" + str(pi2)], att[buf].ap()[:, :],
                           waits=[t_att])
                t_xo = em.dma("sync", xo[buf][:], x_own[128 * i:128 * (i + 1), :],
                              waits=last_w.get(("xo", buf), []))
                t_negm, t_rstd = emit_ln_tile(att[buf][:], sqd2[:], o, [t_att],
                                              war=last_w.get(("tiny", buf), []))
                t_a1 = em.op("vector", lambda v, _b=buf, _o=o: v.tensor_scalar_add(
                    att[_b][:], att[_b][:], tiny[:, _o + 1:_o + 2]), waits=[t_rstd])
                t_xp = em.op("vector", lambda v, _b=buf, _o=o: v.scalar_tensor_tensor(
                    xp[_b][:], att[_b][:], tiny[:, _o + 4:_o + 5], xo[_b][:],
                    ALU.mult, ALU.add),
                    waits=[t_xo] + last_w.get(("xp", buf), []))
                t_sp = em.dma("gpsimd", xp_d[128 * i:128 * (i + 1), :], xp[buf][:],
                              waits=[t_xp])
                t_xp_spill.append(t_sp)
                t_negm3, t_rstd3 = emit_ln_tile(xp[buf][:], sqd2[:], o, [t_xp])
                t_xb2 = em.op("scalar", lambda s, _b=dbuf, _o=o, _xb=buf: s.activation(
                    xb2[_b][:], xp[_xb][:], AF.Identity, bias=tiny[:, _o + 1:_o + 2]),
                    waits=last_w.get(("xb2", dbuf), []))
                t_dg2 = em.op("vector", lambda v, _b=dbuf, _o=o: v.tensor_scalar_mul(
                    diag2[_b][:], identb[:], tiny[:, _o + 4:_o + 5]),
                    waits=[t_rstd3] + last_w.get(("xb2", dbuf), []))
                if pr == 1 and pi2 == 0 and "identw" in debug_taps:
                    em.dma("gpsimd", taps["identw"], identb.ap()[:, :], waits=[t_dg2])
                if pr == 1 and pi2 == 0 and "xb20" in debug_taps:
                    em.dma("gpsimd", taps["xb20"], xb2[dbuf].ap()[:, :], waits=[t_xb2])
                    em.dma("gpsimd", taps["diag20"], diag2[dbuf].ap()[:, :], waits=[t_dg2])
                    em.dma("gpsimd", taps["tinyw"], tiny.ap()[:, :], waits=[t_rstd3, t_xb2])
                t_x2T = emit_transpose16(xb2[dbuf][:], diag2[dbuf][:], x2T_v, i,
                                         t_xb2, t_dg2,
                                         banks=(4 + 2 * pi2, 5 + 2 * pi2))
                last_w[("att", buf)] = [t_xp, t_rstd]
                last_w[("xo", buf)] = [t_xp]
                last_w[("xp", buf)] = [t_sp, t_xb2, t_x2T]
                last_w[("xb2", dbuf)] = [t_x2T]
                last_w[("tiny", buf)] = [t_xb2, t_dg2]
            em.barrier()

    em.barrier()

    tp = tap("x2T", [NC * 128, TO])
    if tp is not None:
        em.dma("gpsimd", tp.rearrange("(k p) t -> p k t", p=128),
               x2T.ap().rearrange("p (k t) -> p k t", k=NC, t=TO))
    tp = tap("xp0", [128, C])
    if tp is not None:
        em.dma("gpsimd", tp, xp_d[0:128, :])
    tp = tap("xp2", [128, C])
    if tp is not None:
        em.dma("gpsimd", tp, xp_d[256:384, :])
    tp = tap("whin", [384, 2 * HF], BF16)
    if tp is not None:
        em.dma("gpsimd", tp, whin_d)

    with ExitStack() as fctx:
        fe = fctx.enter_context
        wgb = [fe(nc.sbuf_tensor(f"f_wgb{i}", [128, NC * 512], BF16)) for i in range(2)]
        hact = [fe(nc.sbuf_tensor(f"f_h{i}", [128, HF], BF16)) for i in range(2)]
        hnT = [fe(nc.sbuf_tensor(f"f_hnT{i}", [128, 64 * 128], BF16)) for i in range(2)]
        sgt = fe(nc.sbuf_tensor("f_sg", [128, 512], F32))
        sqd4 = fe(nc.sbuf_tensor("f_sqd4", [128, C], F32))
        diag4 = fe(nc.sbuf_tensor("f_diag4", [128, 128], BF16))
        wpb = [fe(nc.sbuf_tensor(f"f_wpb{i}", [128, C], BF16)) for i in range(2)]
        prj = [fe(nc.sbuf_tensor(f"f_prj{i}", [128, C], F32)) for i in range(2)]
        xr = [fe(nc.sbuf_tensor("f_xr0", [128, C], F32))] * 2
        fin = [fe(nc.sbuf_tensor("f_fin0", [128, C], F32))] * 2

        em.wait("sync", [t_ccW])

        def g_rhs_ap(k, hchunk):
            r = k // 2
            row0 = 384 * r + 128 * (k % 2)
            return wG_d[row0:row0 + 128, 512 * hchunk:512 * (hchunk + 1)]

        def p_rhs_ap(kk):
            r = kk // 8
            blk = wG_d[384 * r + 256: 384 * r + 384, :].rearrange(
                "a (b c) -> (a b) c", b=8, c=2048)
            return blk[128 * (kk % 8):128 * (kk % 8 + 1), :]

        wg_ld = {}
        wg_war = {}
        hn_rd = {}
        wp_war = {}
        fin_war = {}
        hact_rd = {}
        S_sgt = [None]
        rms_war = [[]]
        for sup in range(4):
            t_hact = {}
            for pj in range(16):
                slots = (0, 1)
                for half, hch in ((0, pj), (1, pj + 16)):
                    slot = slots[half]
                    tls = []
                    for k in range(NC):
                        w0 = wg_war.get(slot, []) if k == 0 else []
                        tls.append(em.dma("sync", wgb[slot][:, 512 * k:512 * (k + 1)],
                                          g_rhs_ap(k, hch), waits=w0))
                    wg_ld[slot] = em.cover(tls)
                for tt in range(2):
                    i = 2 * sup + tt
                    gbk = tt % 2
                    ubk = 2 + gbk
                    t_gu = {}
                    for half, slot, bk_ in ((0, slots[0], gbk), (1, slots[1], ubk)):
                        tmm = None
                        for k in range(NC):
                            w0 = []
                            if k == 0:
                                w0 = wg_ld[slot] + em.bank_wait(bk_)
                            tmm = em.op("tensor", lambda t, _i=i, _k=k, _s=slot,
                                        _bk=bk_: t.matmul(
                                P[_bk][:, 0:512],
                                x2T[:, TO * _k + 128 * (_i % NTO): TO * _k + 128 * (_i % NTO) + 128],
                                wgb[_s][:, 512 * _k:512 * (_k + 1)],
                                start=(_k == 0), stop=(_k == NC - 1)), waits=w0)
                        t_gu[half] = tmm
                    if tt == 1:
                        wg_war[slots[0]] = [t_gu[0]]
                        wg_war[slots[1]] = [t_gu[1]]
                    t_si = em.op("scalar", lambda s, _bk=gbk: s.activation(
                        sgt[:], P[_bk][:, 0:512], AF.Silu),
                        waits=[t_gu[0]] + ([S_sgt[0]] if S_sgt[0] else []))
                    em.bank_read(gbk, t_si)
                    t_h = em.op("vector", lambda v, _tt=tt, _pj=pj, _bk=ubk: v.tensor_tensor(
                        hact[_tt][:, 512 * _pj:512 * (_pj + 1)], P[_bk][:, 0:512],
                        sgt[:], ALU.mult),
                        waits=[t_si, t_gu[1]] + (hact_rd.get(tt, []) if pj == 0 else []))
                    em.bank_read(ubk, t_h)
                    S_sgt[0] = t_h
                    t_hact[tt] = t_h
            t_hn = {}
            for tt in range(2):
                i = 2 * sup + tt
                t_sqs = []
                for qc in range(4):
                    t_sqs.append(em.op("scalar", lambda s, _tt=tt, _q=qc: s.activation(
                        sqd4[:], hact[_tt][:, 2048 * _q:2048 * (_q + 1)],
                        AF.Square, accum_out=tiny[:, 16 + _q:17 + _q]),
                        waits=[t_hact[tt]] + (rms_war[0] if qc == 0 else [])))
                t_sq = t_sqs[-1]
                t_s1 = em.op("vector", lambda v: v.tensor_add(
                    tiny[:, 21:22], tiny[:, 16:17], tiny[:, 17:18]),
                    waits=[t_sq] + rms_war[0])
                t_s2 = em.op("vector", lambda v: v.tensor_add(
                    tiny[:, 22:23], tiny[:, 18:19], tiny[:, 19:20]),
                    waits=[t_s1])
                t_s3 = em.op("vector", lambda v: v.tensor_add(
                    tiny[:, 21:22], tiny[:, 21:22], tiny[:, 22:23]),
                    waits=[t_s2])
                t_v1 = em.op("vector", lambda v: v.tensor_scalar(
                    tiny[:, 19:20], tiny[:, 21:22], 1.0 / HF, RMS_EPS,
                    ALU.mult, ALU.add), waits=[t_s3])
                t_ln = em.op("scalar", lambda s: s.activation(
                    tiny[:, 23:24], tiny[:, 19:20], AF.Ln), waits=[t_v1])
                t_rstd = em.op("scalar", lambda s: s.activation(
                    tiny[:, 20:21], tiny[:, 23:24], AF.Exp, scale=-0.5),
                    waits=[t_ln])
                t_d4 = em.op("vector", lambda v: v.tensor_scalar_mul(
                    diag4[:], identb[:], tiny[:, 20:21]), waits=[t_rstd])
                rms_war[0] = [t_d4]
                hslot = tt % 2
                t_hnT = None
                for k4 in range(16):
                    bk_ = 4 + (k4 % 2)
                    tmm = None
                    for kk in range(4):
                        k = 4 * k4 + kk
                        w0 = []
                        if kk == 0:
                            w0 = [t_d4] + em.bank_wait(bk_)
                            if k4 == 0:
                                w0 += hn_rd.get(hslot, [])
                        tmm = em.op("tensor", lambda t, _tt=tt, _k=k, _kk=kk,
                                    _bk=bk_: t.matmul(
                            P[_bk][:, 128 * _kk:128 * (_kk + 1)],
                            hact[_tt][:, 128 * _k:128 * (_k + 1)], diag4[:],
                            start=True, stop=True), waits=w0)
                    th = em.op("vector", lambda v, _h=hslot, _k4=k4, _bk=bk_: v.tensor_copy(
                        hnT[_h][:, 512 * _k4:512 * (_k4 + 1)], P[_bk][:, 0:512]),
                        waits=[tmm])
                    em.bank_read(bk_, th)
                    t_hnT = th
                t_hn[hslot] = t_hnT
                hact_rd[tt] = [t_hnT, t_sq]
                if tt % 2 == 1:
                    pair = (i - 1, i)
                    t_proj = None
                    for kk in range(64):
                        ws = kk % 2
                        t_wl = em.dma("sync", wpb[ws][:], p_rhs_ap(kk),
                                      waits=wp_war.get(("b", ws), []))
                        t_wc = t_wl
                        for pi2 in range(2):
                            for dc in range(4):
                                bk_ = 4 * pi2 + dc
                                w0 = [t_wc]
                                if kk == 0:
                                    w0 = w0 + [t_hn[pi2]] + em.bank_wait(bk_)
                                t_proj = em.op("tensor", lambda t, _hs=pi2, _kk=kk,
                                               _dc=dc, _w=ws, _bk=bk_: t.matmul(
                                    P[_bk][:, 0:512],
                                    hnT[_hs][:, 128 * _kk:128 * (_kk + 1)],
                                    wpb[_w][:, 512 * _dc:512 * (_dc + 1)],
                                    start=(_kk == 0), stop=(_kk == 63),
                                    skip_group_check=True), waits=w0)
                        wp_war[("b", ws)] = [t_proj]
                    hn_rd[0] = [t_proj]
                    hn_rd[1] = [t_proj]
                    for pi2, ti2 in enumerate(pair):
                        fb = 0
                        o = 8 * (ti2 % 2)
                        t_pe = None
                        for dc in range(4):
                            bk_ = 4 * pi2 + dc
                            t_pe = em.op("vector", lambda v, _f=fb, _dc=dc, _bk=bk_: v.tensor_copy(
                                prj[_f][:, 512 * _dc:512 * (_dc + 1)], P[_bk][:, 0:512]),
                                waits=[t_proj] + (fin_war.get(("prj", fb), [])
                                                  if dc == 0 else []))
                            em.bank_read(bk_, t_pe)
                        t_xr = em.dma("sync", xr[fb][:],
                                      xp_d[128 * ti2:128 * (ti2 + 1), :],
                                      waits=[t_xp_spill[ti2]] + fin_war.get(("xr", fb), []))
                        t_negm5, t_rstd5 = emit_ln_tile(
                            prj[fb][:], sqd4[:], o, [t_pe],
                            war=fin_war.get(("tiny", fb), []))
                        t_p1 = em.op("vector", lambda v, _f=fb, _o=o: v.tensor_scalar_add(
                            prj[_f][:], prj[_f][:], tiny[:, _o + 1:_o + 2]),
                            waits=[t_rstd5])
                        t_fin = em.op("vector", lambda v, _f=fb, _o=o: v.scalar_tensor_tensor(
                            fin[_f][:], prj[_f][:], tiny[:, _o + 4:_o + 5], xr[_f][:],
                            ALU.mult, ALU.add),
                            waits=[t_xr] + fin_war.get(("fin", fb), []))
                        t_st = em.dma("gpsimd", out_d[128 * ti2:128 * (ti2 + 1), :],
                                      fin[fb][:], waits=[t_fin])
                        fin_war[("prj", fb)] = [t_p1, t_rstd5]
                        fin_war[("xr", fb)] = [t_fin]
                        fin_war[("fin", fb)] = [t_st]
                        fin_war[("tiny", fb)] = [t_fin]

    tp = tap("x2Tend", [NC * 128, TO], BF16)
    if tp is not None:
        em.barrier()
        em.dma("gpsimd", tp.rearrange("(k p) t -> p k t", p=128),
               x2T.ap().rearrange("p (k t) -> p k t", k=NC, t=TO))

    em.wait("gpsimd", em.dtoks("gpsimd"))

    em.finalize()
    ctx.close()
    return nc, taps


def kernel(L0_attn_x, L0_attn_Wq, L0_attn_bq, L0_attn_Wk, L0_attn_bk,
           L0_attn_Wv, L0_attn_bv, L0_attn_Wo, L0_attn_bo,
           L0_ffn_w_gate_up, L0_ffn_w_proj):
    x = np.ascontiguousarray(np.asarray(L0_attn_x, dtype=np.float32))
    Wq = np.asarray(L0_attn_Wq, np.float32)
    Wk = np.asarray(L0_attn_Wk, np.float32)
    Wv = np.asarray(L0_attn_Wv, np.float32)
    bq = np.asarray(L0_attn_bq, np.float32)
    bk = np.asarray(L0_attn_bk, np.float32)
    bv = np.asarray(L0_attn_bv, np.float32)
    Wo = np.ascontiguousarray(np.asarray(L0_attn_Wo, np.float32))
    bo = np.asarray(L0_attn_bo, np.float32)
    WguT = np.ascontiguousarray(np.asarray(L0_ffn_w_gate_up, np.float32).T)
    WpT = np.ascontiguousarray(np.asarray(L0_ffn_w_proj, np.float32).T)

    nc, _taps = build_nc(DEBUG_TAPS)
    in_maps = build_in_maps(x, Wq, bq, Wk, bk, Wv, bv, Wo, bo, WguT, WpT)
    res = run_bass_kernel_spmd(nc, in_maps, core_ids=list(range(8)))
    out = np.empty((B, T, C), np.float32)
    for c in range(8):
        b, g = c // 2, c % 2
        out[b, TO * g:TO * (g + 1)] = res.results[c]["out"]
    return out


def build_in_maps(x, Wq, bq, Wk, bk, Wv, bv, Wo, bo, WguT, WpT):
    in_maps = []
    for c in range(8):
        b, g = c // 2, c % 2
        hs = slice(8 * g, 8 * (g + 1))
        m01 = np.zeros((128, 2), np.float32)
        m01[:, 0] = 1.0 - g
        m01[:, 1] = float(g)
        in_maps.append({
            "x_all": x[b],
            "x_own": np.ascontiguousarray(x[b, TO * g:TO * (g + 1)]),
            "wq": np.ascontiguousarray(Wq[hs].reshape(HL * C, D)),
            "wk": np.ascontiguousarray(Wk[hs].reshape(HL * C, D)),
            "wv": np.ascontiguousarray(Wv[hs].transpose(1, 0, 2).reshape(C, HL * D)),
            "bq_t": np.ascontiguousarray(bq[hs].T),
            "bk_t": np.ascontiguousarray(bk[hs].T),
            "bv_r": np.ascontiguousarray(bv[hs].reshape(1, HL * D)),
            "wo": Wo,
            "bo_r": np.ascontiguousarray(bo.reshape(1, C)),
            "wgu_sh": np.ascontiguousarray(WguT[256 * c:256 * (c + 1)]),
            "wp_sh": np.ascontiguousarray(WpT[1024 * c:1024 * (c + 1)]),
            "m01": m01,
        })
    return in_maps


# version 02
# speedup vs baseline: 1.1426x; 1.1426x over previous
import numpy as np
import ml_dtypes
import concourse.bass as bass
import concourse.mybir as mybir
from concourse.bass_utils import run_bass_kernel_spmd
from contextlib import ExitStack

F32 = mybir.dt.float32
BF16 = mybir.dt.bfloat16
AF = mybir.ActivationFunctionType
ALU = mybir.AluOpType
AX = mybir.AxisListType
bf16 = ml_dtypes.bfloat16

B, T, C = 4, 2048, 2048
H, D = 16, 128
HL = 8
TO = 1024
HF = 8192
NT = T // 128
NTO = TO // 128
NC = C // 128
SCALE = float(D) ** -0.5
LN_EPS = 1e-5
RMS_EPS = 1.1920929e-07

DEBUG_TAPS = ()

ENGS = ("sync", "tensor", "vector", "scalar", "gpsimd")


NLANE = 8


class Em:
    def __init__(self, nc):
        self.nc = nc
        self.q = {en: [] for en in ENGS}
        self.clk = {}
        self.dclk = {}
        self.dma_n = {}
        for en in ("tensor", "vector", "scalar", "gpsimd"):
            self.clk[en] = [nc.alloc_semaphore(f"clk_{en}"), 0]
        for en in ("sync", "gpsimd", "vector", "scalar"):
            self.dclk[en] = [[nc.alloc_semaphore(f"dma_{en}{i}"), 0]
                             for i in range(NLANE)]
            self.dma_n[en] = 0
        self.bank_reader = {}

    def tok(self, en):
        return ("c", en, 0, self.clk[en][1])

    def dtoks(self, en):
        return [("d", en, i, self.dclk[en][i][1]) for i in range(NLANE)
                if self.dclk[en][i][1] > 0]

    def _sem_val(self, t):
        kind, en, lane, v = t
        sem = self.clk[en][0] if kind == "c" else self.dclk[en][lane][0]
        return sem, v

    @staticmethod
    def cover(tokens):
        best = {}
        for t in tokens:
            if t is None:
                continue
            k = (t[0], t[1], t[2])
            if k not in best or best[k][3] < t[3]:
                best[k] = t
        return list(best.values())

    def op(self, en, builder, waits=(), inc=True):
        waits = list(waits)
        if en in ("vector", "scalar", "gpsimd") and self.clk[en][1] > 0:
            waits.append(("c", en, 0, self.clk[en][1]))
        if inc:
            self.clk[en][1] += 1
        mytok = ("c", en, 0, self.clk[en][1])
        sem = self.clk[en][0]
        ws = [self._sem_val(t) for t in self.cover(waits) if t[3] > 0]

        def fn(e, _b=builder, _ws=ws, _inc=inc, _sem=sem):
            for s, v in _ws:
                e.wait_ge(s, v)
            ins = _b(e)
            if _inc:
                ins.then_inc(_sem, 1)
        self.q[en].append(fn)
        return mytok

    def dma(self, en, out, in_, waits=()):
        lane = self.dma_n[en] % NLANE
        self.dma_n[en] += 1
        waits = list(waits)
        if self.dclk[en][lane][1] > 0:
            waits.append(("d", en, lane, self.dclk[en][lane][1]))
        self.dclk[en][lane][1] += 16
        mytok = ("d", en, lane, self.dclk[en][lane][1])
        sem = self.dclk[en][lane][0]
        ws = [self._sem_val(t) for t in self.cover(waits) if t[3] > 0]

        def fn(e, _o=out, _i=in_, _ws=ws, _sem=sem):
            for s, v in _ws:
                e.wait_ge(s, v)
            e.dma_start(out=_o, in_=_i).then_inc(_sem, 16)
        self.q[en].append(fn)
        return mytok

    def wait(self, en, tokens):
        ws = [self._sem_val(t) for t in self.cover(tokens) if t[3] > 0]
        if not ws:
            return

        def fn(e, _ws=ws):
            for s, v in _ws:
                e.wait_ge(s, v)
        self.q[en].append(fn)

    def barrier(self):
        toks = [self.tok(en) for en in ("tensor", "vector", "scalar", "gpsimd")]
        for en in ("sync", "gpsimd", "vector", "scalar"):
            toks += self.dtoks(en)
        for en in ENGS:
            mine = [t for t in toks if not (t[1] == en and t[0] == "c")]
            self.wait(en, mine)

    def bank_wait(self, bk):
        r = self.bank_reader.get(bk)
        return [r] if r is not None else []

    def bank_read(self, bk, tok):
        self.bank_reader[bk] = tok

    def finalize(self):
        with self.nc.Block() as block:
            for name in ENGS:
                ops = self.q[name]
                if not ops:
                    continue
                def section(eng, _ops=ops):
                    for f in _ops:
                        f(eng)
                getattr(block, name)(section)


def _consts():
    ident = np.eye(128, dtype=np.float32).astype(bf16)
    ones_col = np.ones((128, 1), np.float32).astype(bf16)
    ones_rowf = np.ones((1, 128), np.float32)
    onesf = np.ones((1, 1), np.float32)
    E1 = np.zeros((128, 2), np.float32)
    E1[0:64, 0] = 1.0
    E1[64:128, 1] = 1.0
    EB = np.ascontiguousarray(E1.T)
    masks = np.zeros((128, 4 * 512), np.float32)
    p = np.arange(128)[:, None]
    f = np.arange(512)[None, :]
    for k in range(4):
        masks[:, 512 * k:512 * (k + 1)] = (p <= f - 128 * k)
    return ident, ones_col, ones_rowf, onesf, E1.astype(bf16), EB.astype(bf16), masks.astype(bf16)


def build_nc(debug_taps=()):
    nc = bass.Bass("TRN2", target_bir_lowering=False, debug=False,
                   dynamic_dma_scratch_size=16384)
    em = Em(nc)

    x_all = nc.dram_tensor("x_all", [T, C], F32, kind="ExternalInput").ap()
    x_own = nc.dram_tensor("x_own", [TO, C], F32, kind="ExternalInput").ap()
    wq_in = nc.dram_tensor("wq", [HL * C, D], F32, kind="ExternalInput").ap()
    wk_in = nc.dram_tensor("wk", [HL * C, D], F32, kind="ExternalInput").ap()
    wv_in = nc.dram_tensor("wv", [C, HL * D], F32, kind="ExternalInput").ap()
    bq_in = nc.dram_tensor("bq_t", [D, HL], F32, kind="ExternalInput").ap()
    bk_in = nc.dram_tensor("bk_t", [D, HL], F32, kind="ExternalInput").ap()
    bv_in = nc.dram_tensor("bv_r", [1, HL * D], F32, kind="ExternalInput").ap()
    wo_in = nc.dram_tensor("wo", [C, C], F32, kind="ExternalInput").ap()
    bo_in = nc.dram_tensor("bo_r", [1, C], F32, kind="ExternalInput").ap()
    wgu_in = nc.dram_tensor("wgu_sh", [256, 2 * HF], F32, kind="ExternalInput").ap()
    wp_in = nc.dram_tensor("wp_sh", [1024, C], F32, kind="ExternalInput").ap()
    m01_in = nc.dram_tensor("m01", [128, 2], F32, kind="ExternalInput").ap()
    out_d = nc.dram_tensor("out", [TO, C], F32, kind="ExternalOutput").ap()

    whin_d = nc.dram_tensor("whin", [384, 2 * HF], BF16, kind="Internal").ap()
    wG_d = nc.dram_tensor("wG", [8 * 384, 2 * HF], BF16, kind="Internal").ap()
    yin_d = nc.dram_tensor("yin", [HL * D, T], BF16, kind="Internal").ap()
    yG_d = nc.dram_tensor("yG", [C, T], BF16, kind="Internal").ap()
    xp_d = nc.dram_tensor("xp_spill", [TO, C], F32, kind="Internal").ap()

    ident_np, onescol_np, onesrowf_np, onesf_np, E1_np, EB_np, masks_np = _consts()
    cident = nc.inline_tensor(ident_np, name="c_ident")
    conescol = nc.inline_tensor(onescol_np, name="c_onescol")
    conesrowf = nc.inline_tensor(onesrowf_np, name="c_onesrowf")
    conesf = nc.inline_tensor(onesf_np, name="c_onesf")
    cE1 = nc.inline_tensor(E1_np, name="c_E1")
    cEB = nc.inline_tensor(EB_np, name="c_EB")
    cmasks = nc.inline_tensor(masks_np, name="c_masks")

    taps = {}
    cc_sem = nc.alloc_semaphore("cc_done")

    def tap(name, shape, dtype=F32):
        if name in debug_taps:
            taps[name] = nc.dram_tensor("tap_" + name, list(shape), dtype,
                                        kind="ExternalOutput").ap()
        return taps.get(name)

    ctx = ExitStack()
    e = ctx.enter_context

    identb = e(nc.sbuf_tensor("identb", [128, 128], BF16))
    onescol = e(nc.sbuf_tensor("onescol", [128, 1], BF16))
    onesrowf = e(nc.sbuf_tensor("onesrowf", [1, 128], F32))
    onesf_s = e(nc.sbuf_tensor("onesf", [1, 1], F32))
    E1b = e(nc.sbuf_tensor("E1b", [128, 2], BF16))
    EBb = e(nc.sbuf_tensor("EBb", [2, 128], BF16))
    masksb = e(nc.sbuf_tensor("masksb", [128, 4 * 512], BF16))
    m01_s = e(nc.sbuf_tensor("m01_sb", [128, 2], F32))
    bqs_s = e(nc.sbuf_tensor("bqs", [128, HL], F32))
    bk_s = e(nc.sbuf_tensor("bk", [128, HL], F32))
    bvf_s = e(nc.sbuf_tensor("bvf", [128, HL * D], F32))
    bof_s = e(nc.sbuf_tensor("bof", [128, C], F32))
    tiny = e(nc.sbuf_tensor("tiny", [128, 24], F32))
    rows_ctx = ExitStack()
    bvrow = rows_ctx.enter_context(nc.sbuf_tensor("bvrow", [1, HL * D], F32))
    borow = rows_ctx.enter_context(nc.sbuf_tensor("borow", [1, C], F32))

    P = [e(nc.psum_tensor(f"p{i}", [128, 512], F32)) for i in range(8)]

    const_toks = []
    for dst, src_ in ((identb[:], cident.ap()), (onescol[:], conescol.ap()),
                      (onesrowf[:], conesrowf.ap()), (onesf_s[:], conesf.ap()),
                      (E1b[:], cE1.ap()), (EBb[:], cEB.ap()),
                      (masksb[:], cmasks.ap()), (m01_s[:], m01_in),
                      (bqs_s[:], bq_in), (bk_s[:], bk_in),
                      (bvrow[:], bv_in), (borow[:], bo_in)):
        const_toks.append(em.dma("sync", dst, src_))
    t_c_all = em.cover(const_toks)
    em.op("vector", lambda v: v.tensor_scalar_mul(bqs_s[:], bqs_s[:], SCALE),
          waits=t_c_all)
    t_bias_mms = []
    for j in range(2):
        tm = em.op("tensor", lambda t, _j=j: t.matmul(
            P[0][:, 0:512], onesrowf[:], bvrow[0:1, 512 * _j:512 * (_j + 1)],
            start=True, stop=True), waits=t_c_all + em.bank_wait(0))
        t_bias_mms.append(tm)
        tv = em.op("vector", lambda v, _j=j: v.tensor_copy(
            bvf_s[:, 512 * _j:512 * (_j + 1)], P[0][:, 0:512]), waits=[tm])
        em.bank_read(0, tv)
    for j in range(4):
        tm = em.op("tensor", lambda t, _j=j: t.matmul(
            P[1][:, 0:512], onesrowf[:], borow[0:1, 512 * _j:512 * (_j + 1)],
            start=True, stop=True), waits=t_c_all + em.bank_wait(1))
        t_bias_mms.append(tm)
        tv = em.op("vector", lambda v, _j=j: v.tensor_copy(
            bof_s[:, 512 * _j:512 * (_j + 1)], P[1][:, 0:512]), waits=[tm])
        em.bank_read(1, tv)

    rows_ctx.close()

    with ExitStack() as qctx:
        qe = qctx.enter_context
        wf = [qe(nc.sbuf_tensor(f"q_wf{i}", [128, 4096], F32)) for i in range(2)]
        ab2 = qe(nc.sbuf_tensor("q_ab2", [128, 4096], BF16))
        sg = qe(nc.sbuf_tensor("q_sg", [128, 4096], BF16))
        sbc = qe(nc.sbuf_tensor("q_sbc", [128, 4096], BF16))
        qt1 = qe(nc.sbuf_tensor("q_t1", [128, 4096], BF16))
        qt2 = [qe(nc.sbuf_tensor(f"q_t2{i}", [128, 4096], BF16)) for i in range(2)]
        scb = qe(nc.sbuf_tensor("q_scb", [2, 4096], BF16))

        panels = []
        for cc in range(2):
            for pp in range(4):
                panels.append((wgu_in[128 * cc:128 * (cc + 1), 4096 * pp:4096 * (pp + 1)],
                               whin_d[128 * cc:128 * (cc + 1), 4096 * pp:4096 * (pp + 1)],
                               4096))
        for kk in range(8):
            dst = whin_d[256 + 16 * kk:256 + 16 * (kk + 1), :].rearrange(
                "a (b c) -> (a b) c", b=8, c=2048)
            panels.append((wp_in[128 * kk:128 * (kk + 1), :], dst, 2048))

        hist = []
        for pi, (src, dst, W) in enumerate(panels):
            buf = pi % 2
            h = {}
            war = ([hist[pi - 2]["sg"], hist[pi - 2]["ab2"]] if pi >= 2
                   else [t_bias_mms[-1]])
            t_ld = em.dma("sync", wf[buf][:, 0:W], src, waits=war)
            prev = hist[-1] if hist else None
            em.op("vector", lambda v, _b=buf, _W=W: v.tensor_scalar_mul(
                qt1[:, 0:_W], wf[_b][:, 0:_W], 2.0),
                waits=[t_ld] + ([prev["qs"]] if prev else []))
            h["ab2"] = em.op("vector", lambda v, _W=W: v.scalar_tensor_tensor(
                ab2[:, 0:_W], qt1[:, 0:_W], -1.0, qt1[:, 0:_W],
                ALU.mult, ALU.max))
            h["sg"] = em.op("scalar", lambda s, _b=buf, _W=W: s.activation(
                sg[:, 0:_W], wf[_b][:, 0:_W], AF.Sign),
                waits=[t_ld] + ([prev["what"]] if prev else []))
            nseg = W // 512
            t_bc_list = []
            for sgi in range(nseg):
                sl = slice(512 * sgi, 512 * (sgi + 1))
                bk_ = sgi % 2
                tm = em.op("tensor", lambda t, _sl=sl, _bk=bk_: t.matmul(
                    P[_bk][0:2, 0:512], E1b[:], ab2[:, _sl], start=True, stop=True),
                    waits=[h["ab2"]] + em.bank_wait(bk_))
                t_sc = em.op("vector", lambda v, _sl=sl, _bk=bk_: v.tensor_scalar(
                    scb[0:2, _sl], P[_bk][0:2, 0:512], 1.0 / 128.0, 1e-8,
                    ALU.mult, ALU.max), waits=[tm])
                em.bank_read(bk_, t_sc)
                bk2 = 2 + sgi % 2
                tm2 = em.op("tensor", lambda t, _sl=sl, _bk=bk2: t.matmul(
                    P[_bk][:, 0:512], EBb[:], scb[0:2, _sl], start=True, stop=True),
                    waits=[t_sc] + em.bank_wait(bk2))
                t_bc = em.op("vector", lambda v, _sl=sl, _bk=bk2: v.tensor_copy(
                    sbc[:, _sl], P[_bk][:, 0:512]), waits=[tm2])
                em.bank_read(bk2, t_bc)
                t_bc_list.append(t_bc)
            h["qa"] = em.op("vector", lambda v, _W=W: v.tensor_tensor(
                qt1[:, 0:_W], ab2[:, 0:_W], sbc[:, 0:_W], ALU.is_gt),
                waits=[t_bc_list[-1], h["ab2"]])
            h["qs"] = em.op("vector", lambda v, _W=W: v.tensor_tensor(
                qt1[:, 0:_W], qt1[:, 0:_W], sbc[:, 0:_W], ALU.mult),
                waits=[h["qa"]])
            st_war = [hist[pi - 2]["st"]] if pi >= 2 else []
            h["what"] = em.op("vector", lambda v, _b=buf, _W=W: v.tensor_tensor(
                qt2[_b][:, 0:_W], qt1[:, 0:_W], sg[:, 0:_W], ALU.mult),
                waits=[h["sg"], h["qs"]] + st_war)
            h["st"] = em.dma("gpsimd", dst, qt2[buf][:, 0:W], waits=[h["what"]])
            hist.append(h)

        em.op("gpsimd", lambda g: g.collective_compute(
            "AllGather", ALU.bypass, replica_groups=[list(range(8))],
            ins=[whin_d], outs=[wG_d]).then_inc(cc_sem, 1),
            waits=[hist[-1]["st"], hist[-2]["st"]], inc=False)

    em.barrier()

    laq = ExitStack()
    x1T = laq.enter_context(nc.sbuf_tensor("x1T", [128, NC * T], BF16))
    yT = laq.enter_context(nc.sbuf_tensor("yT", [128, HL * T], BF16))
    x1T_v = x1T.ap().rearrange("p (k t) -> p k t", k=NC, t=T)

    def emit_ln_tile(src_tile, sqd, col0, waits0, war=()):
        o = col0
        war = list(war)
        t_sum = em.op("vector", lambda v, _s=src_tile: v.tensor_reduce(
            tiny[:, o + 0:o + 1], _s, AX.X, ALU.add), waits=list(waits0) + war)
        t_negm = em.op("vector", lambda v: v.tensor_scalar_mul(
            tiny[:, o + 1:o + 2], tiny[:, o + 0:o + 1], -1.0 / C),
            waits=[t_sum])
        t_sq = em.op("scalar", lambda s, _s=src_tile, _d=sqd: s.activation(
            _d, _s, AF.Square, accum_out=tiny[:, o + 5:o + 6]),
            waits=list(waits0) + war)
        t_m2 = em.op("scalar", lambda s: s.activation(
            tiny[:, o + 2:o + 3], tiny[:, o + 1:o + 2], AF.Square),
            waits=[t_negm])
        t_v1 = em.op("vector", lambda v: v.scalar_tensor_tensor(
            tiny[:, o + 3:o + 4], tiny[:, o + 5:o + 6], 1.0 / C,
            tiny[:, o + 2:o + 3], ALU.mult, ALU.subtract),
            waits=[t_m2, t_sq])
        t_v2 = em.op("vector", lambda v: v.tensor_scalar_add(
            tiny[:, o + 3:o + 4], tiny[:, o + 3:o + 4], LN_EPS),
            waits=[t_v1])
        t_ln = em.op("scalar", lambda s: s.activation(
            tiny[:, o + 6:o + 7], tiny[:, o + 3:o + 4], AF.Ln), waits=[t_v2])
        t_rstd = em.op("scalar", lambda s: s.activation(
            tiny[:, o + 4:o + 5], tiny[:, o + 6:o + 7], AF.Exp, scale=-0.5),
            waits=[t_ln])
        return t_negm, t_rstd

    def emit_transpose16(src_b, diag_b, dst_view, i, t_src, t_diag, banks=(4, 5)):
        out_toks = []
        for k4 in range(4):
            bk_ = banks[k4 % 2]
            tmm = None
            for kk in range(4):
                k = 4 * k4 + kk
                w0 = ([t_src, t_diag] + em.bank_wait(bk_)) if kk == 0 else []
                tmm = em.op("tensor", lambda t, _k=k, _kk=kk, _bk=bk_, _s=src_b,
                            _dg=diag_b: t.matmul(
                    P[_bk][:, 128 * _kk:128 * (_kk + 1)],
                    _s[:, 128 * _k:128 * (_k + 1)], _dg[:],
                    start=True, stop=True), waits=w0)
            dst = dst_view[:, 4 * k4:4 * (k4 + 1), 128 * i:128 * (i + 1)]
            srcp = P[bk_].ap().rearrange("p (k t) -> p k t", k=4, t=128)
            tv = em.op("vector", lambda v, _d=dst, _sp=srcp: v.tensor_copy(_d, _sp),
                       waits=[tmm])
            em.bank_read(bk_, tv)
            out_toks.append(tv)
        return out_toks[-1]

    with ExitStack() as lctx:
        le = lctx.enter_context
        xt = [le(nc.sbuf_tensor(f"l_x{i}", [128, C], F32)) for i in range(2)]
        sqd = le(nc.sbuf_tensor("l_sqd", [128, C], F32))
        xb = [le(nc.sbuf_tensor(f"l_xb{i}", [128, C], BF16)) for i in range(2)]
        diag = [le(nc.sbuf_tensor(f"l_diag{i}", [128, 128], BF16)) for i in range(2)]

        last_use = {}
        tiny_war = {}
        t_x1T = None
        for i in range(NT):
            buf = i % 2
            war = last_use.get(buf, [])
            t_x = em.dma("sync", xt[buf][:], x_all[128 * i:128 * (i + 1), :],
                         waits=war)
            t_negm, t_rstd = emit_ln_tile(xt[buf][:], sqd[:], 8 * buf, [t_x],
                                          war=tiny_war.get(buf, []))
            t_xb = em.op("scalar", lambda s, _b=buf, _o=8 * buf: s.activation(
                xb[_b][:], xt[_b][:], AF.Identity, bias=tiny[:, _o + 1:_o + 2]))
            t_dg = em.op("vector", lambda v, _b=buf, _o=8 * buf: v.tensor_scalar_mul(
                diag[_b][:], identb[:], tiny[:, _o + 4:_o + 5]), waits=[t_rstd])
            t_x1T = emit_transpose16(xb[buf][:], diag[buf][:], x1T_v, i, t_xb, t_dg)
            last_use[buf] = [t_xb, t_x1T]
            tiny_war[buf] = [t_xb, t_dg]

    tp = tap("x1T", [NC * 128, T])
    if tp is not None:
        em.dma("gpsimd", tp.rearrange("(k p) t -> p k t", p=128),
               x1T.ap().rearrange("p (k t) -> p k t", k=NC, t=T),
               waits=[t_x1T])

    em.barrier()

    tap("aPT", [128, 2048], BF16)
    tap("aPTM", [128, 2048], BF16)
    tap("ar2", [1, 512], F32)
    tap("aysc", [128, 512], BF16)
    tap("aqb", [128, T], BF16)
    tap("akb", [128, T], BF16)
    tap("avq", [128, NT * 512], BF16)
    with ExitStack() as actx:
        ae = actx.enter_context
        vq = ae(nc.sbuf_tensor("a_vq", [128, NT * 512], BF16))
        wvf = [ae(nc.sbuf_tensor(f"a_wvf{i}", [128, 512], F32)) for i in range(2)]
        wvb = ae(nc.sbuf_tensor("a_wvb", [128, NC * 512], BF16))
        stg = [ae(nc.sbuf_tensor(f"a_stg{i}", [128, 512], F32)) for i in range(2)]
        wqb = ae(nc.sbuf_tensor("a_wqb", [128, 2048], BF16))
        wkb = ae(nc.sbuf_tensor("a_wkb", [128, 2048], BF16))
        qb = [ae(nc.sbuf_tensor("a_qb0", [128, T], BF16))] * 2
        kb = [ae(nc.sbuf_tensor("a_kb0", [128, T], BF16))] * 2
        PT = [ae(nc.sbuf_tensor("a_PT0", [128, 12 * 512], BF16)),
              ae(nc.sbuf_tensor("a_PT1", [128, 16 * 512], BF16))]
        PTM = ae(nc.sbuf_tensor("a_PTM", [128, 4 * 512], BF16))
        r2row = ae(nc.sbuf_tensor("a_r2row", [1, 512], F32))
        r2col = ae(nc.sbuf_tensor("a_r2col", [128, 4], F32))
        ysc = ae(nc.sbuf_tensor("a_ysc", [128, 512], BF16))

        S = {"vq_rd": None, "pt_war": {}, "ptm_war": None, "sum_war": None,
             "r2_war": None, "ysc_war": None, "yt_war": None,
             "y_war": {}, "qk_rd": {}, "wv_cast": []}

        def emit_vquad(qd, extra_wait=None, stage_war=()):
            casts = []
            for k in range(NC):
                buf = k % 2
                war = [casts[k - 2]] if k >= 2 else list(stage_war)
                t_ld = em.dma("sync", wvf[buf][:],
                              wv_in[128 * k:128 * (k + 1), 512 * qd:512 * (qd + 1)],
                              waits=war)
                tc = em.op("vector", lambda v, _b=buf, _k=k: v.tensor_copy(
                    wvb[:, 512 * _k:512 * (_k + 1)], wvf[_b][:]), waits=[t_ld])
                casts.append(tc)
            t_wv = casts[-1]
            S["wv_cast"] = casts[-2:]
            t_last = None
            for j in range(NT):
                tmm = None
                for k in range(NC):
                    w0 = []
                    if k == 0:
                        w0 = [t_wv] + em.bank_wait(7)
                        if j == 0 and extra_wait is not None:
                            w0.append(extra_wait)
                    tmm = em.op("tensor", lambda t, _j=j, _k=k: t.matmul(
                        P[7][:, 0:512],
                        x1T[:, T * _k + 128 * _j: T * _k + 128 * (_j + 1)],
                        wvb[:, 512 * _k:512 * (_k + 1)],
                        start=(_k == 0), stop=(_k == NC - 1)), waits=w0)
                t_last = em.op("vector", lambda v, _j=j, _qd=qd: v.scalar_tensor_tensor(
                    vq[:, 512 * _j:512 * (_j + 1)], P[7][:, 0:512], 1.0,
                    bvf_s[:, 512 * _qd:512 * (_qd + 1)], ALU.mult, ALU.add),
                    waits=[tmm])
                em.bank_read(7, t_last)
            return t_last

        def emit_qk(hl, buf):
            t_casts = {}
            for wsel, (w_in_, wb_dst) in enumerate(((wq_in, wqb), (wk_in, wkb))):
                for qtr in range(4):
                    sb = qtr % 2
                    r0 = hl * C + 512 * qtr
                    t_l = em.dma(
                        "sync",
                        stg[sb].ap().rearrange("p (k d) -> p k d", k=4, d=D),
                        w_in_[r0:r0 + 512, :].rearrange("(k p) d -> p k d", p=128),
                        waits=S["qk_rd"].get(("stg", sb), []))
                    tc = em.op("vector", lambda v, _sb=sb, _d=wb_dst, _q=qtr: v.tensor_copy(
                        _d[:, 512 * _q:512 * (_q + 1)], stg[_sb][:]),
                        waits=[t_l] + (S["qk_rd"].get(("wb", wsel), [])
                                       if qtr == 0 else []))
                    S["qk_rd"][("stg", sb)] = [tc]
                    t_casts[wsel] = tc
            t_cq = t_casts[0]
            t_ck = t_casts[1]
            t_out = None
            for isq, (wb_, outb, t_cc) in enumerate(((wqb, qb[buf], t_cq),
                                                     (wkb, kb[buf], t_ck))):
                for j in range(4):
                    bk_ = 4 + (j % 2)
                    tmm = None
                    for k in range(NC):
                        w0 = ([t_cc] + em.bank_wait(bk_) +
                              S["qk_rd"].get("qk", [])) if k == 0 else []
                        tmm = em.op("tensor", lambda t, _wb=wb_, _j=j, _k=k, _bk=bk_: t.matmul(
                            P[_bk][:, 0:512],
                            _wb[:, 128 * _k:128 * (_k + 1)],
                            x1T[:, T * _k + 512 * _j: T * _k + 512 * (_j + 1)],
                            start=(_k == 0), stop=(_k == NC - 1)), waits=w0)
                    if isq == 0:
                        tv = em.op("vector", lambda v, _o=outb, _j=j, _bk=bk_, _h=hl: v.tensor_scalar(
                            _o[:, 512 * _j:512 * (_j + 1)], P[_bk][:, 0:512],
                            SCALE, bqs_s[:, _h:_h + 1], ALU.mult, ALU.add),
                            waits=[tmm])
                    else:
                        tv = em.op("vector", lambda v, _o=outb, _j=j, _bk=bk_, _h=hl: v.tensor_scalar_add(
                            _o[:, 512 * _j:512 * (_j + 1)], P[_bk][:, 0:512],
                            bk_s[:, _h:_h + 1]), waits=[tmm])
                    em.bank_read(bk_, tv)
                    t_out = tv
                S["qk_rd"][("wb", isq)] = [tmm]
            return t_out

        def emit_attn(hl, buf, t_qk, t_v):
            hq = hl % 4
            scores_rd = []
            for c in range(4):
                pt_id = c % 2
                ptb = PT[pt_id]
                njt = 4 * c + 4
                t_exp = {}
                t_msk = {}
                for j in range(njt):
                    bk_ = j % 2
                    w0 = em.bank_wait(bk_)
                    if j == 0:
                        w0 = w0 + [t_qk] + S["pt_war"].get(pt_id, [])
                    tmm = em.op("tensor", lambda t, _j=j, _c=c, _bk=bk_, _b=buf: t.matmul(
                        P[_bk][:, 0:512],
                        kb[_b][:, 128 * _j:128 * (_j + 1)],
                        qb[_b][:, 512 * _c:512 * (_c + 1)],
                        start=True, stop=True), waits=w0)
                    scores_rd.append(tmm)
                    te = em.op("scalar", lambda s, _j=j, _bk=bk_, _pt=ptb: s.activation(
                        _pt[:, 512 * _j:512 * (_j + 1)], P[_bk][:, 0:512], AF.Exp),
                        waits=[tmm])
                    em.bank_read(bk_, te)
                    t_exp[j] = te
                    if j >= 4 * c:
                        kv = j - 4 * c
                        wm = [t_exp[j]]
                        if S["ptm_war"] is not None and kv == 0:
                            wm.append(S["ptm_war"])
                        tmk = em.op("vector", lambda v, _j=j, _kv=kv, _pt=ptb: v.tensor_tensor(
                            PTM[:, 512 * _kv:512 * (_kv + 1)],
                            _pt[:, 512 * _j:512 * (_j + 1)],
                            masksb[:, 512 * _kv:512 * (_kv + 1)], ALU.mult),
                            waits=wm)
                        t_msk[j] = tmk
                ybk = 3 + (c % 2)
                t_sumpv = None
                for j in range(njt):
                    if j >= 4 * c:
                        srcPT, off, wj = PTM, 512 * (j - 4 * c), [t_msk[j]]
                    else:
                        srcPT, off, wj = ptb, 512 * j, [t_exp[j]]
                    w0 = wj if j > 0 else wj + em.bank_wait(2) + (
                        [S["sum_war"]] if S["sum_war"] else [])
                    em.op("tensor", lambda t, _s=srcPT, _o=off, _j=j, _n=njt: t.matmul(
                        P[2][0:1, 0:512], onescol[:], _s[:, _o:_o + 512],
                        start=(_j == 0), stop=(_j == _n - 1),
                        skip_group_check=True), waits=w0)
                for tt in range(4):
                    for j in range(njt):
                        if j >= 4 * c:
                            srcPT, off = PTM, 512 * (j - 4 * c)
                        else:
                            srcPT, off = ptb, 512 * j
                        w1 = []
                        if j == 0 and tt == 0:
                            w1 = [t_v] + em.bank_wait(ybk)
                        t_sumpv = em.op("tensor", lambda t, _s=srcPT, _o=off, _tt=tt,
                                        _j=j, _n=njt, _bk=ybk, _hq=hq: t.matmul(
                            P[_bk][:, 128 * _tt:128 * (_tt + 1)],
                            _s[:, _o + 128 * _tt:_o + 128 * (_tt + 1)],
                            vq[:, 512 * _j + 128 * _hq: 512 * _j + 128 * (_hq + 1)],
                            start=(_j == 0), stop=(_j == _n - 1),
                            skip_group_check=True), waits=w1)
                S["pt_war"][pt_id] = [t_sumpv]
                S["ptm_war"] = t_sumpv
                t_r2 = em.op("vector", lambda v: v.reciprocal(
                    r2row[0:1, :], P[2][0:1, 0:512]), waits=[t_sumpv])
                em.bank_read(2, t_r2)
                t_r2t = None
                for tt in range(4):
                    w0 = []
                    if tt == 0:
                        w0 = [t_r2] + em.bank_wait(5)
                    t_r2t = em.op("tensor", lambda t, _tt=tt: t.matmul(
                        P[5][:, _tt:_tt + 1],
                        r2row[0:1, 128 * _tt:128 * (_tt + 1)], onesf_s[:],
                        start=True, stop=True), waits=w0)
                t_r2c = em.op("vector", lambda v: v.tensor_copy(
                    r2col[:, 0:4], P[5][:, 0:4]), waits=[t_r2t])
                em.bank_read(5, t_r2c)
                t_ysc = None
                for tt in range(4):
                    w0 = [t_r2c]
                    if tt == 0 and S["ysc_war"]:
                        w0 = w0 + [S["ysc_war"]]
                    t_ysc = em.op("vector", lambda v, _tt=tt, _bk=ybk: v.tensor_scalar_mul(
                        ysc[:, 128 * _tt:128 * (_tt + 1)],
                        P[_bk][:, 128 * _tt:128 * (_tt + 1)],
                        r2col[:, _tt:_tt + 1]), waits=w0)
                em.bank_read(ybk, t_ysc)
                t_yt = None
                for tt in range(4):
                    w0 = []
                    if tt == 0:
                        w0 = [t_ysc] + em.bank_wait(6)
                    t_yt = em.op("tensor", lambda t, _tt=tt: t.matmul(
                        P[6][:, 128 * _tt:128 * (_tt + 1)],
                        ysc[:, 128 * _tt:128 * (_tt + 1)], identb[:],
                        start=True, stop=True), waits=w0)
                S["ysc_war"] = t_yt
                t_ye = em.op("vector", lambda v, _hl=hl, _c=c: v.tensor_copy(
                    yT[:, T * _hl + 512 * _c: T * _hl + 512 * (_c + 1)],
                    P[6][:, 0:512]), waits=[t_yt])
                em.bank_read(6, t_ye)
                S["sum_war"] = t_r2c
                if hl == 0 and c == 0 and "aPT" in debug_taps:
                    tdm1 = em.dma("gpsimd", taps["aPT"], PT[0].ap()[:, 0:2048],
                                  waits=[t_exp[3]])
                    tdm2 = em.dma("gpsimd", taps["aPTM"], PTM.ap()[:, 0:2048],
                                  waits=[t_msk[3]])
                    tdm3 = em.dma("gpsimd", taps["ar2"], r2row.ap()[0:1, :],
                                  waits=[t_r2])
                    tdm4 = em.dma("gpsimd", taps["aysc"], ysc.ap()[:, 0:512],
                                  waits=[t_ysc])
                    tdm5 = em.dma("gpsimd", taps["aqb"], qb[0].ap()[:, :],
                                  waits=[t_qk])
                    tdm6 = em.dma("gpsimd", taps["akb"], kb[0].ap()[:, :],
                                  waits=[t_qk])
                    tdm7 = em.dma("gpsimd", taps["avq"], vq.ap()[:, :],
                                  waits=[t_v])
                    S["ptm_war"] = tdm2
                    S["pt_war"][0] = [t_sumpv, tdm1]
                    S["sum_war"] = tdm3
                    S["ysc_war"] = tdm4
            S["qk_rd"]["qk"] = [scores_rd[-1]]
            return t_ye, t_sumpv

        t_v = emit_vquad(0)
        t_qk = emit_qk(0, 0)
        t_vq_rd = None
        t_ye = None
        for hl in range(HL):
            t_ye, t_pv = emit_attn(hl, hl % 2, t_qk, t_v)
            t_vq_rd = t_pv
            if hl == 3:
                t_v = emit_vquad(1, extra_wait=t_vq_rd, stage_war=S["wv_cast"])
            if hl + 1 < HL:
                t_qk = emit_qk(hl + 1, (hl + 1) % 2)

    em.barrier()

    tp = tap("yT", [HL * 128, T])
    if tp is not None:
        em.dma("gpsimd", tp.rearrange("(h p) t -> p h t", p=128),
               yT.ap().rearrange("p (h t) -> p h t", h=HL, t=T))

    t_yst = em.dma("gpsimd", yin_d.rearrange("(h p) t -> p h t", p=128),
                   yT.ap().rearrange("p (h t) -> p h t", h=HL, t=T))
    em.barrier()
    laq.close()
    em.op("gpsimd", lambda g: g.collective_compute(
        "AllGather", ALU.bypass, replica_groups=[[0, 1], [2, 3], [4, 5], [6, 7]],
        ins=[yin_d], outs=[yG_d]).then_inc(cc_sem, 1), waits=[t_yst], inc=False)


    tap("ysel", [NC * 128, TO], BF16)
    tap("g00", [128, TO], BF16)
    tap("g10", [128, TO], BF16)
    tap("xb20", [128, C], BF16)
    tap("diag20", [128, 128], BF16)
    tap("tinyw", [128, 24])
    tap("identw", [128, 128], BF16)
    tap("att0", [128, C])
    tap("att1", [128, C])
    x2T = e(nc.sbuf_tensor("x2T", [128, NC * TO], BF16))
    x2T_v = x2T.ap().rearrange("p (k t) -> p k t", k=NC, t=TO)

    t_xp_spill = []
    with ExitStack() as wctx:
        we = wctx.enter_context
        wof = [we(nc.sbuf_tensor(f"w_wof{i}", [128, C], F32)) for i in range(2)]
        wob = [we(nc.sbuf_tensor(f"w_wob{i}", [128, C], BF16)) for i in range(2)]
        g0 = [we(nc.sbuf_tensor(f"w_g0{i}", [128, TO], BF16)) for i in range(2)]
        g1 = [we(nc.sbuf_tensor(f"w_g1{i}", [128, TO], BF16)) for i in range(2)]
        gt = we(nc.sbuf_tensor("w_gt", [128, TO], BF16))
        ysel = we(nc.sbuf_tensor("w_ysel", [128, NC * TO], BF16))
        att = [we(nc.sbuf_tensor(f"w_att{i}", [128, C], F32)) for i in range(2)]
        xo = [we(nc.sbuf_tensor(f"w_xo{i}", [128, C], F32)) for i in range(2)]
        xp = [we(nc.sbuf_tensor(f"w_xp{i}", [128, C], F32)) for i in range(2)]
        sqd2 = we(nc.sbuf_tensor("w_sqd", [128, C], F32))
        xb2 = [we(nc.sbuf_tensor(f"w_xb2{i}", [128, C], BF16)) for i in range(4)]
        diag2 = [we(nc.sbuf_tensor(f"w_diag2{i}", [128, 128], BF16)) for i in range(4)]

        em.q["gpsimd"].append(lambda e: e.wait_ge(cc_sem, 2))
        gsel = []
        for k in range(NC):
            buf = k % 2
            war = [gsel[k - 2]] if k >= 2 else []
            t0 = em.dma("gpsimd", g0[buf][:], yG_d[128 * k:128 * (k + 1), 0:TO],
                        waits=war)
            t1 = em.dma("gpsimd", g1[buf][:], yG_d[128 * k:128 * (k + 1), TO:T])
            if k == 0 and "g00" in debug_taps:
                tg1 = em.dma("gpsimd", taps["g00"], g0[0].ap()[:, :], waits=[t0])
                tg2 = em.dma("gpsimd", taps["g10"], g1[0].ap()[:, :], waits=[t1])
            tgt = em.op("vector", lambda v, _b=buf: v.tensor_scalar_mul(
                gt[:], g0[_b][:], m01_s[:, 0:1]), waits=[t0])
            ts = em.op("vector", lambda v, _b=buf, _k=k: v.scalar_tensor_tensor(
                ysel[:, TO * _k:TO * (_k + 1)], g1[_b][:], m01_s[:, 1:2], gt[:],
                ALU.mult, ALU.add), waits=[t1])
            gsel.append(ts)
        t_ysel = gsel[-1]
        if "ysel" in debug_taps:
            em.dma("gpsimd", taps["ysel"].rearrange("(k p) t -> p k t", p=128),
                   ysel.ap().rearrange("p (k t) -> p k t", k=NC, t=TO),
                   waits=[t_ysel])

        last_w = {}
        wo_war = {}
        for pr in range(4):
            t_wo_mm = None
            for k in range(NC):
                ws = k % 2
                t_l = em.dma("sync", wof[ws][:], wo_in[128 * k:128 * (k + 1), :],
                             waits=wo_war.get(("f", ws), []))
                t_cw = em.op("vector", lambda v, _w=ws: v.tensor_copy(
                    wob[_w][:], wof[_w][:]),
                    waits=[t_l] + wo_war.get(("b", ws), []))
                wo_war[("f", ws)] = [t_cw]
                for pi2 in range(2):
                    i = 2 * pr + pi2
                    for ec in range(4):
                        bk_ = 4 * pi2 + ec
                        w0 = [t_cw] if (pi2 == 0 and ec == 0) else []
                        if k == 0:
                            w0 = w0 + [t_ysel] + em.bank_wait(bk_)
                        t_wo_mm = em.op("tensor", lambda t, _i=i, _ec=ec, _k=k,
                                        _bk=bk_, _w=ws: t.matmul(
                            P[_bk][:, 0:512],
                            ysel[:, TO * _k + 128 * _i: TO * _k + 128 * (_i + 1)],
                            wob[_w][:, 512 * _ec: 512 * (_ec + 1)],
                            start=(_k == 0), stop=(_k == NC - 1),
                            skip_group_check=True), waits=w0)
                wo_war[("b", ws)] = [t_wo_mm]
            t_att_p = {}
            for pi2 in range(2):
                buf = pi2
                t_att = None
                for ec in range(4):
                    bk_ = 4 * pi2 + ec
                    t_att = em.op("vector", lambda v, _b=buf, _ec=ec, _bk=bk_: v.scalar_tensor_tensor(
                        att[_b][:, 512 * _ec:512 * (_ec + 1)], P[_bk][:, 0:512], 1.0,
                        bof_s[:, 512 * _ec:512 * (_ec + 1)], ALU.mult, ALU.add),
                        waits=[t_wo_mm] + (last_w.get(("att", buf), [])
                                           if ec == 0 else []))
                    em.bank_read(bk_, t_att)
                t_att_p[pi2] = t_att
            for pi2 in range(2):
                i = 2 * pr + pi2
                buf = pi2
                dbuf = 2 * (pr % 2) + pi2
                o = 8 * buf
                t_att = t_att_p[pi2]
                if pr == 0 and ("att" + str(pi2)) in debug_taps:
                    em.dma("gpsimd", taps["att" + str(pi2)], att[buf].ap()[:, :],
                           waits=[t_att])
                t_xo = em.dma("sync", xo[buf][:], x_own[128 * i:128 * (i + 1), :],
                              waits=last_w.get(("xo", buf), []))
                t_negm, t_rstd = emit_ln_tile(att[buf][:], sqd2[:], o, [t_att],
                                              war=last_w.get(("tiny", buf), []))
                t_a1 = em.op("vector", lambda v, _b=buf, _o=o: v.tensor_scalar_add(
                    att[_b][:], att[_b][:], tiny[:, _o + 1:_o + 2]), waits=[t_rstd])
                t_xp = em.op("vector", lambda v, _b=buf, _o=o: v.scalar_tensor_tensor(
                    xp[_b][:], att[_b][:], tiny[:, _o + 4:_o + 5], xo[_b][:],
                    ALU.mult, ALU.add),
                    waits=[t_xo] + last_w.get(("xp", buf), []))
                t_sp = em.dma("gpsimd", xp_d[128 * i:128 * (i + 1), :], xp[buf][:],
                              waits=[t_xp])
                t_xp_spill.append(t_sp)
                t_negm3, t_rstd3 = emit_ln_tile(xp[buf][:], sqd2[:], o, [t_xp])
                t_xb2 = em.op("scalar", lambda s, _b=dbuf, _o=o, _xb=buf: s.activation(
                    xb2[_b][:], xp[_xb][:], AF.Identity, bias=tiny[:, _o + 1:_o + 2]),
                    waits=last_w.get(("xb2", dbuf), []))
                t_dg2 = em.op("vector", lambda v, _b=dbuf, _o=o: v.tensor_scalar_mul(
                    diag2[_b][:], identb[:], tiny[:, _o + 4:_o + 5]),
                    waits=[t_rstd3] + last_w.get(("xb2", dbuf), []))
                if pr == 1 and pi2 == 0 and "identw" in debug_taps:
                    em.dma("gpsimd", taps["identw"], identb.ap()[:, :], waits=[t_dg2])
                if pr == 1 and pi2 == 0 and "xb20" in debug_taps:
                    em.dma("gpsimd", taps["xb20"], xb2[dbuf].ap()[:, :], waits=[t_xb2])
                    em.dma("gpsimd", taps["diag20"], diag2[dbuf].ap()[:, :], waits=[t_dg2])
                    em.dma("gpsimd", taps["tinyw"], tiny.ap()[:, :], waits=[t_rstd3, t_xb2])
                t_x2T = emit_transpose16(xb2[dbuf][:], diag2[dbuf][:], x2T_v, i,
                                         t_xb2, t_dg2,
                                         banks=(4 + 2 * pi2, 5 + 2 * pi2))
                last_w[("att", buf)] = [t_xp, t_rstd]
                last_w[("xo", buf)] = [t_xp]
                last_w[("xp", buf)] = [t_sp, t_xb2, t_x2T]
                last_w[("xb2", dbuf)] = [t_x2T]
                last_w[("tiny", buf)] = [t_xb2, t_dg2]

    em.barrier()

    tp = tap("x2T", [NC * 128, TO])
    if tp is not None:
        em.dma("gpsimd", tp.rearrange("(k p) t -> p k t", p=128),
               x2T.ap().rearrange("p (k t) -> p k t", k=NC, t=TO))
    tp = tap("xp0", [128, C])
    if tp is not None:
        em.dma("gpsimd", tp, xp_d[0:128, :])
    tp = tap("xp2", [128, C])
    if tp is not None:
        em.dma("gpsimd", tp, xp_d[256:384, :])
    tp = tap("whin", [384, 2 * HF], BF16)
    if tp is not None:
        em.dma("gpsimd", tp, whin_d)

    with ExitStack() as fctx:
        fe = fctx.enter_context
        wgb = [fe(nc.sbuf_tensor(f"f_wgb{i}", [128, NC * 512], BF16)) for i in range(2)]
        hact = [fe(nc.sbuf_tensor(f"f_h{i}", [128, HF], BF16)) for i in range(2)]
        hnT = [fe(nc.sbuf_tensor(f"f_hnT{i}", [128, 64 * 128], BF16)) for i in range(2)]
        sgt = fe(nc.sbuf_tensor("f_sg", [128, 512], F32))
        sqd4 = fe(nc.sbuf_tensor("f_sqd4", [128, C], F32))
        diag4 = fe(nc.sbuf_tensor("f_diag4", [128, 128], BF16))
        wpb = [fe(nc.sbuf_tensor(f"f_wpb{i}", [128, C], BF16)) for i in range(2)]
        prj = [fe(nc.sbuf_tensor(f"f_prj{i}", [128, C], F32)) for i in range(2)]
        xr = [fe(nc.sbuf_tensor("f_xr0", [128, C], F32))] * 2
        fin = [fe(nc.sbuf_tensor("f_fin0", [128, C], F32))] * 2

        em.q["sync"].append(lambda e: e.wait_ge(cc_sem, 1))

        def g_rhs_ap(k, hchunk):
            r = k // 2
            row0 = 384 * r + 128 * (k % 2)
            return wG_d[row0:row0 + 128, 512 * hchunk:512 * (hchunk + 1)]

        def p_rhs_ap(kk):
            r = kk // 8
            blk = wG_d[384 * r + 256: 384 * r + 384, :].rearrange(
                "a (b c) -> (a b) c", b=8, c=2048)
            return blk[128 * (kk % 8):128 * (kk % 8 + 1), :]

        wg_ld = {}
        wg_war = {}
        hn_rd = {}
        wp_war = {}
        fin_war = {}
        hact_rd = {}
        S_sgt = [None]
        rms_war = [[]]
        for sup in range(4):
            t_hact = {}
            for pj in range(16):
                slots = (0, 1)
                G4 = wG_d.rearrange("(a x p) h -> p a x h", a=8, x=3, p=128)
                for half, hch in ((0, pj), (1, pj + 16)):
                    slot = slots[half]
                    dstv = wgb[slot].ap().rearrange("p (a x h) -> p a x h",
                                                    a=8, x=2, h=512)
                    tls = []
                    for xx in range(2):
                        tls.append(em.dma(
                            "sync", dstv[:, :, xx, :],
                            G4[:, :, xx, 512 * hch:512 * (hch + 1)],
                            waits=wg_war.get(slot, []) if xx == 0 else []))
                    wg_ld[slot] = em.cover(tls)
                for tt in range(2):
                    i = 2 * sup + tt
                    gbk = tt % 2
                    ubk = 2 + gbk
                    t_gu = {}
                    for half, slot, bk_ in ((0, slots[0], gbk), (1, slots[1], ubk)):
                        tmm = None
                        for k in range(NC):
                            w0 = []
                            if k == 0:
                                w0 = wg_ld[slot] + em.bank_wait(bk_)
                            tmm = em.op("tensor", lambda t, _i=i, _k=k, _s=slot,
                                        _bk=bk_: t.matmul(
                                P[_bk][:, 0:512],
                                x2T[:, TO * _k + 128 * (_i % NTO): TO * _k + 128 * (_i % NTO) + 128],
                                wgb[_s][:, 512 * _k:512 * (_k + 1)],
                                start=(_k == 0), stop=(_k == NC - 1)), waits=w0)
                        t_gu[half] = tmm
                    if tt == 1:
                        wg_war[slots[0]] = [t_gu[0]]
                        wg_war[slots[1]] = [t_gu[1]]
                    t_si = em.op("scalar", lambda s, _bk=gbk: s.activation(
                        sgt[:], P[_bk][:, 0:512], AF.Silu),
                        waits=[t_gu[0]] + ([S_sgt[0]] if S_sgt[0] else []))
                    em.bank_read(gbk, t_si)
                    t_h = em.op("vector", lambda v, _tt=tt, _pj=pj, _bk=ubk: v.tensor_tensor(
                        hact[_tt][:, 512 * _pj:512 * (_pj + 1)], P[_bk][:, 0:512],
                        sgt[:], ALU.mult),
                        waits=[t_si, t_gu[1]] + (hact_rd.get(tt, []) if pj == 0 else []))
                    em.bank_read(ubk, t_h)
                    S_sgt[0] = t_h
                    t_hact[tt] = t_h
            t_hn = {}
            for tt in range(2):
                i = 2 * sup + tt
                t_sqs = []
                for qc in range(4):
                    t_sqs.append(em.op("scalar", lambda s, _tt=tt, _q=qc: s.activation(
                        sqd4[:], hact[_tt][:, 2048 * _q:2048 * (_q + 1)],
                        AF.Square, accum_out=tiny[:, 16 + _q:17 + _q]),
                        waits=[t_hact[tt]] + (rms_war[0] if qc == 0 else [])))
                t_sq = t_sqs[-1]
                t_s1 = em.op("vector", lambda v: v.tensor_add(
                    tiny[:, 21:22], tiny[:, 16:17], tiny[:, 17:18]),
                    waits=[t_sq] + rms_war[0])
                t_s2 = em.op("vector", lambda v: v.tensor_add(
                    tiny[:, 22:23], tiny[:, 18:19], tiny[:, 19:20]),
                    waits=[t_s1])
                t_s3 = em.op("vector", lambda v: v.tensor_add(
                    tiny[:, 21:22], tiny[:, 21:22], tiny[:, 22:23]),
                    waits=[t_s2])
                t_v1 = em.op("vector", lambda v: v.tensor_scalar(
                    tiny[:, 19:20], tiny[:, 21:22], 1.0 / HF, RMS_EPS,
                    ALU.mult, ALU.add), waits=[t_s3])
                t_ln = em.op("scalar", lambda s: s.activation(
                    tiny[:, 23:24], tiny[:, 19:20], AF.Ln), waits=[t_v1])
                t_rstd = em.op("scalar", lambda s: s.activation(
                    tiny[:, 20:21], tiny[:, 23:24], AF.Exp, scale=-0.5),
                    waits=[t_ln])
                t_d4 = em.op("vector", lambda v: v.tensor_scalar_mul(
                    diag4[:], identb[:], tiny[:, 20:21]), waits=[t_rstd])
                rms_war[0] = [t_d4]
                hslot = tt % 2
                t_hnT = None
                for k4 in range(16):
                    bk_ = 4 + (k4 % 2)
                    tmm = None
                    for kk in range(4):
                        k = 4 * k4 + kk
                        w0 = []
                        if kk == 0:
                            w0 = [t_d4] + em.bank_wait(bk_)
                            if k4 == 0:
                                w0 += hn_rd.get(hslot, [])
                        tmm = em.op("tensor", lambda t, _tt=tt, _k=k, _kk=kk,
                                    _bk=bk_: t.matmul(
                            P[_bk][:, 128 * _kk:128 * (_kk + 1)],
                            hact[_tt][:, 128 * _k:128 * (_k + 1)], diag4[:],
                            start=True, stop=True), waits=w0)
                    th = em.op("vector", lambda v, _h=hslot, _k4=k4, _bk=bk_: v.tensor_copy(
                        hnT[_h][:, 512 * _k4:512 * (_k4 + 1)], P[_bk][:, 0:512]),
                        waits=[tmm])
                    em.bank_read(bk_, th)
                    t_hnT = th
                t_hn[hslot] = t_hnT
                hact_rd[tt] = [t_hnT, t_sq]
                if tt % 2 == 1:
                    pair = (i - 1, i)
                    t_proj = None
                    for kk in range(64):
                        ws = kk % 2
                        t_wl = em.dma("sync", wpb[ws][:], p_rhs_ap(kk),
                                      waits=wp_war.get(("b", ws), []))
                        t_wc = t_wl
                        for pi2 in range(2):
                            for dc in range(4):
                                bk_ = 4 * pi2 + dc
                                w0 = [t_wc] if (pi2 == 0 and dc == 0) else []
                                if kk == 0:
                                    w0 = w0 + [t_hn[pi2]] + em.bank_wait(bk_)
                                t_proj = em.op("tensor", lambda t, _hs=pi2, _kk=kk,
                                               _dc=dc, _w=ws, _bk=bk_: t.matmul(
                                    P[_bk][:, 0:512],
                                    hnT[_hs][:, 128 * _kk:128 * (_kk + 1)],
                                    wpb[_w][:, 512 * _dc:512 * (_dc + 1)],
                                    start=(_kk == 0), stop=(_kk == 63),
                                    skip_group_check=True), waits=w0)
                        wp_war[("b", ws)] = [t_proj]
                    hn_rd[0] = [t_proj]
                    hn_rd[1] = [t_proj]
                    for pi2, ti2 in enumerate(pair):
                        fb = 0
                        o = 8 * (ti2 % 2)
                        t_pe = None
                        for dc in range(4):
                            bk_ = 4 * pi2 + dc
                            t_pe = em.op("vector", lambda v, _f=fb, _dc=dc, _bk=bk_: v.tensor_copy(
                                prj[_f][:, 512 * _dc:512 * (_dc + 1)], P[_bk][:, 0:512]),
                                waits=[t_proj] + (fin_war.get(("prj", fb), [])
                                                  if dc == 0 else []))
                            em.bank_read(bk_, t_pe)
                        t_xr = em.dma("sync", xr[fb][:],
                                      xp_d[128 * ti2:128 * (ti2 + 1), :],
                                      waits=[t_xp_spill[ti2]] + fin_war.get(("xr", fb), []))
                        t_negm5, t_rstd5 = emit_ln_tile(
                            prj[fb][:], sqd4[:], o, [t_pe],
                            war=fin_war.get(("tiny", fb), []))
                        t_p1 = em.op("vector", lambda v, _f=fb, _o=o: v.tensor_scalar_add(
                            prj[_f][:], prj[_f][:], tiny[:, _o + 1:_o + 2]),
                            waits=[t_rstd5])
                        t_fin = em.op("vector", lambda v, _f=fb, _o=o: v.scalar_tensor_tensor(
                            fin[_f][:], prj[_f][:], tiny[:, _o + 4:_o + 5], xr[_f][:],
                            ALU.mult, ALU.add),
                            waits=[t_xr] + fin_war.get(("fin", fb), []))
                        t_st = em.dma("gpsimd", out_d[128 * ti2:128 * (ti2 + 1), :],
                                      fin[fb][:], waits=[t_fin])
                        fin_war[("prj", fb)] = [t_p1, t_rstd5]
                        fin_war[("xr", fb)] = [t_fin]
                        fin_war[("fin", fb)] = [t_st]
                        fin_war[("tiny", fb)] = [t_fin]

    tp = tap("x2Tend", [NC * 128, TO], BF16)
    if tp is not None:
        em.barrier()
        em.dma("gpsimd", tp.rearrange("(k p) t -> p k t", p=128),
               x2T.ap().rearrange("p (k t) -> p k t", k=NC, t=TO))

    em.wait("gpsimd", em.dtoks("gpsimd"))

    em.finalize()
    ctx.close()
    return nc, taps


def kernel(L0_attn_x, L0_attn_Wq, L0_attn_bq, L0_attn_Wk, L0_attn_bk,
           L0_attn_Wv, L0_attn_bv, L0_attn_Wo, L0_attn_bo,
           L0_ffn_w_gate_up, L0_ffn_w_proj):
    x = np.ascontiguousarray(np.asarray(L0_attn_x, dtype=np.float32))
    Wq = np.asarray(L0_attn_Wq, np.float32)
    Wk = np.asarray(L0_attn_Wk, np.float32)
    Wv = np.asarray(L0_attn_Wv, np.float32)
    bq = np.asarray(L0_attn_bq, np.float32)
    bk = np.asarray(L0_attn_bk, np.float32)
    bv = np.asarray(L0_attn_bv, np.float32)
    Wo = np.ascontiguousarray(np.asarray(L0_attn_Wo, np.float32))
    bo = np.asarray(L0_attn_bo, np.float32)
    WguT = np.ascontiguousarray(np.asarray(L0_ffn_w_gate_up, np.float32).T)
    WpT = np.ascontiguousarray(np.asarray(L0_ffn_w_proj, np.float32).T)

    nc, _taps = build_nc(DEBUG_TAPS)
    in_maps = build_in_maps(x, Wq, bq, Wk, bk, Wv, bv, Wo, bo, WguT, WpT)
    res = run_bass_kernel_spmd(nc, in_maps, core_ids=list(range(8)))
    out = np.empty((B, T, C), np.float32)
    for c in range(8):
        b, g = c // 2, c % 2
        out[b, TO * g:TO * (g + 1)] = res.results[c]["out"]
    return out


def build_in_maps(x, Wq, bq, Wk, bk, Wv, bv, Wo, bo, WguT, WpT):
    in_maps = []
    for c in range(8):
        b, g = c // 2, c % 2
        hs = slice(8 * g, 8 * (g + 1))
        m01 = np.zeros((128, 2), np.float32)
        m01[:, 0] = 1.0 - g
        m01[:, 1] = float(g)
        in_maps.append({
            "x_all": x[b],
            "x_own": np.ascontiguousarray(x[b, TO * g:TO * (g + 1)]),
            "wq": np.ascontiguousarray(Wq[hs].reshape(HL * C, D)),
            "wk": np.ascontiguousarray(Wk[hs].reshape(HL * C, D)),
            "wv": np.ascontiguousarray(Wv[hs].transpose(1, 0, 2).reshape(C, HL * D)),
            "bq_t": np.ascontiguousarray(bq[hs].T),
            "bk_t": np.ascontiguousarray(bk[hs].T),
            "bv_r": np.ascontiguousarray(bv[hs].reshape(1, HL * D)),
            "wo": Wo,
            "bo_r": np.ascontiguousarray(bo.reshape(1, C)),
            "wgu_sh": np.ascontiguousarray(WguT[256 * c:256 * (c + 1)]),
            "wp_sh": np.ascontiguousarray(WpT[1024 * c:1024 * (c + 1)]),
            "m01": m01,
        })
    return in_maps
